# version 08
# baseline (speedup 1.0000x reference)
import numpy as np
import ml_dtypes
import concourse.bass as bass
import concourse.mybir as mybir
from concourse.bass_utils import run_bass_kernel_spmd

F32 = mybir.dt.float32
BF16 = mybir.dt.bfloat16
I32 = mybir.dt.int32
AF = mybir.ActivationFunctionType
ALU = mybir.AluOpType
AX = mybir.AxisListType

T = 2048
D = 1024
NCHUNK = D // 128
NT = T // 512
NT128 = T // 128
NEG = -1.0e30

FUSED = True
DBG = {}


class Sem:
    def __init__(self, h, name):
        self.h = h
        self.name = name
        self.n = 0

    def at(self):
        return (self, self.n)


class B:
    ENGINES = ("sync", "tensor", "scalar", "vector", "gpsimd")

    def __init__(self, nc):
        self.nc = nc
        self.ops = {e: [] for e in self.ENGINES}
        self.last = {e: {} for e in self.ENGINES}
        self.autow = {}
        self.trace = {}

    def sem(self, name):
        return Sem(self.nc.alloc_semaphore(name), name)

    def op(self, eng, fn, wait=(), inc=None, inc_n=1, tag="", fwait=()):
        lw = self.last[eng]
        if eng in self.autow:
            s_ = self.autow[eng]
            wait = list(wait) + [(s_, s_.n)]
        waits = []
        wmeta = []
        for s, v in fwait:
            waits.append((s.h, v))
            wmeta.append((s.name, v))
            if lw.get(s.name, -1) < v:
                lw[s.name] = v
        for s, v in wait:
            if v > 0 and lw.get(s.name, -1) < v:
                waits.append((s.h, v))
                wmeta.append((s.name, v))
                lw[s.name] = v
        if inc is not None:
            inc.n += inc_n
            ih, iv = inc.h, inc_n
            imeta = (inc.name, inc_n)
        else:
            ih, iv = None, 0
            imeta = None
        self.trace.setdefault(eng, []).append((wmeta, imeta, tag))

        def run(e, fn=fn, waits=tuple(waits), ih=ih, iv=iv):
            for h, v in waits:
                e.wait_ge(h, v)
            ins = fn(e)
            if ih is not None:
                ins.then_inc(ih, iv)
        self.ops[eng].append(run)
        return inc.n if inc is not None else None

    def dma(self, eng, out, in_, wait=(), inc=None):
        return self.op(eng, lambda e, o=out, i=in_: e.dma_start(out=o, in_=i),
                       wait=wait, inc=inc, inc_n=16)

    def barrier(self, sems):
        for eng in self.ENGINES:
            for s in sems:
                if s.n > 0 and self.last[eng].get(s.name, -1) < s.n:
                    self.last[eng][s.name] = s.n
                    self.trace.setdefault(eng, []).append(
                        ([(s.name, s.n)], None, "barrier"))
                    self.ops[eng].append(lambda e, h=s.h, v=s.n: e.wait_ge(h, v))

    def finalize(self, block):
        for eng in self.ENGINES:
            ops = self.ops[eng]
            if not ops:
                continue

            def body(e, ops=ops):
                for f in ops:
                    f(e)
            getattr(block, eng)(body)


def _chunks(ap2d):
    return ap2d.rearrange("(c p) n -> c p n", p=128)


def build_program(blocks):
    nc = bass.Bass("TRN2", target_bir_lowering=False)
    b = B(nc)

    x_h = nc.dram_tensor("x", [T, D], F32, kind="ExternalInput").ap()
    out_h = nc.dram_tensor("out", [T, D], F32, kind="ExternalOutput").ap()
    iden_bf_h = nc.dram_tensor("c_ident_bf", [128, 128], BF16, kind="ExternalInput").ap()
    iden_f_h = nc.dram_tensor("c_ident_f", [128, 128], F32, kind="ExternalInput").ap()
    tri_h = nc.dram_tensor("c_tri", [128, 128], BF16, kind="ExternalInput").ap()
    sel_h = nc.dram_tensor("c_sel", [8, 8 * 128], BF16, kind="ExternalInput").ap()

    W = {}
    if 0 in blocks:
        W["w_attn"] = nc.dram_tensor("L0_attn_w_attn", [D, 3 * D], F32, kind="ExternalInput").ap()
        W["b_attn"] = nc.dram_tensor("L0_attn_b_attn", [3 * D], F32, kind="ExternalInput").ap()
        W["w_proj"] = nc.dram_tensor("L0_attn_w_proj", [D, D], F32, kind="ExternalInput").ap()
        W["b_proj"] = nc.dram_tensor("L0_attn_b_proj", [D], F32, kind="ExternalInput").ap()
    if 1 in blocks:
        W["gate_w"] = nc.dram_tensor("L0_ffn_gate_w", [D, 8], F32, kind="ExternalInput").ap()
        W["w_gate"] = nc.dram_tensor("L0_ffn_w_gate", [8, D, 2048], F32, kind="ExternalInput").ap()
        W["w_up"] = nc.dram_tensor("L0_ffn_w_up", [8, D, 2048], F32, kind="ExternalInput").ap()
        W["w_down"] = nc.dram_tensor("L0_ffn_w_down", [8, 2048, D], F32, kind="ExternalInput").ap()
        W["ws_gate"] = nc.dram_tensor("L0_ffn_ws_gate", [D, 2048], F32, kind="ExternalInput").ap()
        W["ws_up"] = nc.dram_tensor("L0_ffn_ws_up", [D, 2048], F32, kind="ExternalInput").ap()
        W["ws_down"] = nc.dram_tensor("L0_ffn_ws_down", [2048, D], F32, kind="ExternalInput").ap()
    if 2 in blocks:
        W["wk"] = nc.dram_tensor("L1_attn_Wk", [D, D], F32, kind="ExternalInput").ap()
        W["wq"] = nc.dram_tensor("L1_attn_Wq", [D, D], F32, kind="ExternalInput").ap()
        W["wv"] = nc.dram_tensor("L1_attn_Wv", [D, D], F32, kind="ExternalInput").ap()
    if 3 in blocks:
        W["wg_i"] = nc.dram_tensor("L1_ffn_w_gate", [3072, D], I32, kind="ExternalInput").ap()
        W["wu_i"] = nc.dram_tensor("L1_ffn_w_up", [3072, D], I32, kind="ExternalInput").ap()
        W["wd_i"] = nc.dram_tensor("L1_ffn_w_down", [D, 3072], I32, kind="ExternalInput").ap()

    stream = [nc.alloc_sbuf_tensor(f"stream{c}", [128, T], F32) for c in range(NCHUNK)]
    lnx = [nc.alloc_sbuf_tensor(f"lnx{c}", [128, T], BF16) for c in range(NCHUNK)]
    statr = nc.alloc_sbuf_tensor("statr", [128, T], F32)
    iden_bf = nc.alloc_sbuf_tensor("iden_bf", [128, 128], BF16)
    iden_f = nc.alloc_sbuf_tensor("iden_f", [128, 128], F32)
    tri_sb = nc.alloc_sbuf_tensor("tri_sb", [128, 128], BF16)
    sel_sb = nc.alloc_sbuf_tensor("sel_sb", [8, 8 * 128], BF16)
    ones_col_f = nc.alloc_sbuf_tensor("ones_col_f", [128, 1], F32)
    ones_row_bf = nc.alloc_sbuf_tensor("ones_row_bf", [33, 128], BF16)
    onesr_bf = nc.alloc_sbuf_tensor("onesr_bf", [1, 512], BF16)
    onesr_f = nc.alloc_sbuf_tensor("onesr_f", [1, 128], F32)
    ones64_bf = nc.alloc_sbuf_tensor("ones64_bf", [64, 1], BF16)
    onesc_bf = nc.alloc_sbuf_tensor("onesc_bf", [128, 1], BF16)
    ones_11 = nc.alloc_sbuf_tensor("ones_11", [1, 1], F32)
    ones65_f = nc.alloc_sbuf_tensor("ones65_f", [65, 1], F32)
    bproj_sb = nc.alloc_sbuf_tensor("bproj_sb", [128, 8], F32)
    eps_sb = nc.alloc_sbuf_tensor("eps_sb", [1, 1], F32)

    pool_lo = (nc.sbuf_base + 511) & ~511
    pool_hi = nc.sbuf_top - 64

    class Pool:
        def __init__(self, off=None):
            self.off = pool_lo if off is None else off

        def t(self, name, shape, dtype):
            per_part = int(np.prod(shape[1:])) * mybir.dt.size(dtype)
            off = (self.off + 255) & ~255
            assert off + per_part <= pool_hi, (
                f"pool overflow at {name}: {off + per_part} > {pool_hi}")
            h = nc.alloc_sbuf_tensor_at(f"{name}_{nc.next_id()}", list(shape),
                                        dtype, offset=off)
            self.off = off + per_part
            return h

        def fork(self):
            return Pool(self.off)

    pbk = [nc.alloc_psum_tensor(f"pb{i}", [128, 512], F32) for i in range(8)]

    sL = b.sem("sL")
    sG = b.sem("sG")
    sP = b.sem("sP")
    sA = b.sem("sA")
    sV = b.sem("sV")
    ALLSEMS = (sL, sG, sP, sA, sV)
    b.autow["vector"] = sV

    def bar():
        b.barrier(ALLSEMS)

    b.dma("sync", iden_bf[:], iden_bf_h, inc=sL)
    b.dma("sync", iden_f[:], iden_f_h, inc=sL)
    b.dma("sync", tri_sb[:], tri_h, inc=sL)
    b.dma("sync", sel_sb[:], sel_h, inc=sL)
    if 0 in blocks:
        for c in range(8):
            b.dma("sync", bproj_sb[:, c:c + 1],
                  W["b_proj"][c * 128:(c + 1) * 128][:, None], inc=sL)
    b.op("vector", lambda e: e.memset(ones_col_f[:], 1.0), inc=sV)
    b.op("vector", lambda e: e.memset(ones_row_bf[:], 1.0), inc=sV)
    b.op("vector", lambda e: e.memset(onesr_bf[:], 1.0), inc=sV)
    b.op("vector", lambda e: e.memset(onesr_f[:], 1.0), inc=sV)
    b.op("vector", lambda e: e.memset(ones64_bf[:], 1.0), inc=sV)
    b.op("vector", lambda e: e.memset(onesc_bf[:], 1.0), inc=sV)
    b.op("vector", lambda e: e.memset(ones_11[:], 1.0), inc=sV)
    b.op("vector", lambda e: e.memset(ones65_f[:], 1.0), inc=sV)
    b.op("vector", lambda e: e.memset(eps_sb[:], 1e-5), inc=sV)
    bar()

    pool = Pool()
    xin = [pool.t(f"xin{i}", [128, D], F32) for i in range(2)]
    tp_done = []
    for tc in range(NT128):
        buf = xin[tc % 2]
        wt = [(sP, tp_done[tc - 2])] if tc >= 2 else []
        ld = b.dma("sync", buf[:], x_h[tc * 128:(tc + 1) * 128, :], wait=wt, inc=sL)
        for fc in range(NCHUNK):
            bank = pbk[fc % 2]
            b.op("tensor",
                 lambda e, buf=buf, fc=fc, bank=bank: e.matmul(
                     bank[0:128, 0:128], buf[:, fc * 128:(fc + 1) * 128],
                     iden_f[:], start=True, stop=True),
                 wait=[(sL, ld), (sA, max(0, sA.n - 1))], inc=sP)
            b.op("scalar",
                 lambda e, fc=fc, tc=tc, bank=bank: e.activation(
                     out=stream[fc][:, tc * 128:(tc + 1) * 128],
                     in_=bank[0:128, 0:128], func=AF.Copy),
                 wait=[(sP, sP.n)], inc=sA)
        tp_done.append(sP.n)
    bar()

    def emit_ln():
        lpool = Pool()
        sq = lpool.t("ln_sq", [128, 2 * 512], F32)
        rowbf = lpool.t("ln_rowbf", [33, T], BF16)
        msq = lpool.t("ln_msq", [1, 2 * T], F32)
        mb = lpool.t("ln_mb", [128, T], BF16)
        rb = lpool.t("ln_rb", [128, T], BF16)
        base = (sP, sP.n)
        sq_free = [None, None]
        mmd = {}
        for t in range(NT):
            for c in range(NCHUNK):
                slot = sq[:, (c % 2) * 512:(c % 2) * 512 + 512]
                sw = [base]
                if sq_free[c % 2]:
                    sw.append(sq_free[c % 2])
                b.op("scalar",
                     lambda e, c=c, t=t, slot=slot: e.activation(
                         out=slot, in_=stream[c][:, t * 512:(t + 1) * 512],
                         func=AF.Square),
                     wait=sw, inc=sA)
                sqd = sA.at()
                b.op("tensor",
                     lambda e, c=c, t=t: e.matmul(
                         pbk[t][0:1, :], ones_col_f[:],
                         stream[c][:, t * 512:(t + 1) * 512],
                         start=(c == 0), stop=(c == NCHUNK - 1)),
                     wait=[base] if c == 0 else [], inc=sP)
                b.op("tensor",
                     lambda e, c=c, t=t, slot=slot: e.matmul(
                         pbk[4 + t][0:1, :], ones_col_f[:], slot,
                         start=(c == 0), stop=(c == NCHUNK - 1)),
                     wait=[sqd], inc=sP)
                sq_free[c % 2] = (sP, sP.n)
            mmd[t] = sP.at()
            b.op("scalar",
                 lambda e, t=t: e.activation(
                     out=statr[0:1, t * 512:(t + 1) * 512], in_=pbk[t][0:1, :],
                     func=AF.Identity, scale=1.0 / D),
                 wait=[mmd[t]], inc=sA)
            b.op("scalar",
                 lambda e, t=t: e.activation(
                     out=msq[0:1, t * 512:(t + 1) * 512], in_=pbk[4 + t][0:1, :],
                     func=AF.Identity, scale=1.0 / D),
                 inc=sA)
            mmd[t] = sA.at()
        for t in range(NT):
            tc = (t * 512, (t + 1) * 512)
            b.op("vector",
                 lambda e, t=t: e.tensor_mul(
                     msq[0:1, T + t * 512:T + (t + 1) * 512],
                     statr[0:1, t * 512:(t + 1) * 512],
                     statr[0:1, t * 512:(t + 1) * 512]),
                 wait=[mmd[t]], inc=sV)
            b.op("vector",
                 lambda e, t=t: e.tensor_sub(
                     msq[0:1, t * 512:(t + 1) * 512],
                     msq[0:1, t * 512:(t + 1) * 512],
                     msq[0:1, T + t * 512:T + (t + 1) * 512]),
                 inc=sV)
            vd = sV.at()
            b.op("scalar",
                 lambda e, t=t: e.activation(
                     out=statr[64:65, t * 512:(t + 1) * 512],
                     in_=msq[0:1, t * 512:(t + 1) * 512],
                     func=AF.Sqrt, bias=eps_sb[0:1, 0:1]),
                 wait=[vd], inc=sA)
            b.op("vector",
                 lambda e, t=t: e.reciprocal(
                     statr[64:65, t * 512:(t + 1) * 512],
                     statr[64:65, t * 512:(t + 1) * 512]),
                 wait=[sA.at()], inc=sV)
            rec = sV.at()
            b.op("scalar",
                 lambda e, t=t: e.activation(
                     out=rowbf[0:1, t * 512:(t + 1) * 512],
                     in_=statr[0:1, t * 512:(t + 1) * 512], func=AF.Copy),
                 wait=[rec], inc=sA)
            b.op("scalar",
                 lambda e, t=t: e.activation(
                     out=rowbf[32:33, t * 512:(t + 1) * 512],
                     in_=statr[64:65, t * 512:(t + 1) * 512], func=AF.Copy),
                 inc=sA)
            rbf = sA.at()
            for dst, row, bank in ((mb, 0, pbk[t]), (rb, 1, pbk[4 + t])):
                b.op("tensor",
                     lambda e, row=row, t=t, bank=bank: e.matmul(
                         bank[0:128, :], ones_row_bf[32 * row:32 * row + 1, :],
                         rowbf[32 * row:32 * row + 1, t * 512:(t + 1) * 512],
                         start=True, stop=True),
                     wait=[rbf], inc=sP)
                b.op("scalar",
                     lambda e, dst=dst, t=t, bank=bank: e.activation(
                         out=dst[:, t * 512:(t + 1) * 512], in_=bank[0:128, :],
                         func=AF.Copy),
                     wait=[(sP, sP.n)], inc=sA)
            bc = sA.at()
            for c in range(NCHUNK):
                b.op("vector",
                     lambda e, c=c, t=t: e.tensor_sub(
                         lnx[c][:, t * 512:(t + 1) * 512],
                         stream[c][:, t * 512:(t + 1) * 512],
                         mb[:, t * 512:(t + 1) * 512]),
                     wait=[bc] if c == 0 else [], inc=sV)
                b.op("vector",
                     lambda e, c=c, t=t: e.tensor_mul(
                         lnx[c][:, t * 512:(t + 1) * 512],
                         lnx[c][:, t * 512:(t + 1) * 512],
                         rb[:, t * 512:(t + 1) * 512]),
                     inc=sV)
        bar()

    def emit_l0_attn():
        pool = Pool()
        vtm = pool.t("a_vtm", [128, 16 * 520], BF16)
        qa = pool.t("a_qa", [128, T], BF16)
        qb = pool.t("a_qb", [128, T], BF16)
        ka = pool.t("a_ka", [128, T], BF16)
        kb = pool.t("a_kb", [128, T], BF16)
        rows = pool.t("a_rows", [128, T], BF16)
        rows2 = pool.t("a_rows2", [64, T], BF16)
        rws = pool.t("a_rws", [1, 1048], F32)
        ebuf = pool.t("a_ebuf", [128, 4 * 512], BF16)
        dbc = [pool.t(f"a_dbc{i}", [64, 512], F32) for i in range(2)]
        ybf = [pool.t(f"a_ybf{i}", [128, T], BF16) for i in range(4)]
        sqb = pool.t("a_sqb", [64, 2 * 512], BF16)
        hlr = pool.t("a_hlr", [1, 1024], BF16)
        bq2 = pool.t("a_bq2", [128, 2], F32)
        bvr = pool.t("a_bvr", [1, 1024], BF16)
        bvf = pool.t("a_bvf", [1, 1024], F32)
        wstg = pool.t("a_wstg", [128, 512], F32)
        wbf = pool.t("a_wbf", [128, 2048], BF16)
        pX = pool.fork()
        wvstg = pX.t("a_wvstg", [128, 1024], F32)
        wvbf = pX.t("a_wvbf", [128, 8 * 512], BF16)
        pX2 = pool.fork()
        pstg = pX2.t("a_pstg", [128, 512], F32)
        pbf = [pX2.t(f"a_pbf{i}", [128, 512], BF16) for i in range(2)]

        wat = _chunks(W["w_attn"])
        wpt = _chunks(W["w_proj"])

        st = {
            "wv_stg_free": None,
            "w_stg_free": None,
            "wbf_free": None,
            "vtm_free": None,
            "ybf_free": None,
            "pstg_free": None, "pbf_free": [None, None], "pj_bfree": [None, None],
            "bq_free": None,
            "x_free": None,
            "qk_banks": None,
            "b4_free": None,
            "bc_free": [None, None],
            "dbc_free": [None, None],
            "av_free": [None, None],
            "qa_free": None,
            "aug_ld": None,
            "qk_done": None,
        }
        b.op("vector", lambda e: e.memset(ka[64:65, :], 1.0), inc=sV)
        b.op("vector", lambda e: e.memset(kb[64:65, :], 1.0), inc=sV)

        def emit_vb(r):
            b.dma("sync", bvf[0:1, 0:512],
                  W["b_attn"][2 * D + r * 512:2 * D + (r + 1) * 512][None, :],
                  wait=[(sP, sP.n), (sA, sA.n)], inc=sL)
            b.op("scalar", lambda e: e.activation(out=bvr[0:1, 0:512], in_=bvf[0:1, 0:512],
                                                  func=AF.Copy),
                 wait=[sL.at()], inc=sA)
            b.op("vector", lambda e: e.tensor_sub(bvf[0:1, 512:1024], bvf[0:1, 0:512], bvr[0:1, 0:512]),
                 wait=[sA.at()], inc=sV)
            b.op("scalar", lambda e: e.activation(out=bvr[0:1, 512:1024], in_=bvf[0:1, 512:1024],
                                                  func=AF.Copy),
                 wait=[sV.at()], inc=sA)
            bias_d = sA.at()
            vw = [st["vtm_free"]] if st["vtm_free"] else []
            for s in range(16):
                b.op("vector",
                     lambda e, s=s: e.memset(
                         vtm[:, s * 520:(s + 1) * 520].rearrange(
                             "p (h x) -> p h x", x=65)[:, :, 64:65], 1.0),
                     wait=vw if s == 0 else [], inc=sV)
            ones_d = sV.at()
            for qtr in range(4):
                wt = [st["wv_stg_free"]] if st["wv_stg_free"] else []
                if qtr == 0 and st["x_free"]:
                    wt = wt + [st["x_free"]]
                b.dma("sync", wvstg[:].rearrange("p (c n) -> p c n", n=128),
                      wat[:, :, 2 * D + r * 512 + qtr * 128:
                          2 * D + r * 512 + (qtr + 1) * 128].rearrange("c p n -> p c n"),
                      wait=wt, inc=sL)
                cwv = [sL.at()]
                if qtr == 0:
                    cwv.append((sP, sP.n))
                b.op("scalar",
                     lambda e, qtr=qtr: e.activation(
                         out=wvbf[:].rearrange("p (c n) -> p c n", n=512)[
                             :, :, qtr * 128:(qtr + 1) * 128],
                         in_=wvstg[:].rearrange("p (c n) -> p c n", n=128),
                         func=AF.Copy),
                     wait=cwv, inc=sA)
                st["wv_stg_free"] = sA.at()
            cast_d = sA.at()
            vb_bank_free = [None, None]
            for t16 in range(NT128):
                bank = pbk[6 + t16 % 2]
                wt0 = [cast_d, (sV, sV.n)]
                if vb_bank_free[t16 % 2]:
                    wt0.append(vb_bank_free[t16 % 2])
                for c in range(NCHUNK):
                    b.op("tensor",
                         lambda e, c=c, t16=t16, bank=bank: e.matmul(
                             bank[:, :], lnx[c][:, t16 * 128:(t16 + 1) * 128],
                             wvbf[:, c * 512:(c + 1) * 512],
                             start=(c == 0), stop=False),
                         wait=wt0 if c == 0 else [], inc=sP)
                b.op("tensor",
                     lambda e, bank=bank: e.matmul(
                         bank[:, :], ones_row_bf[0:1, 0:128], bvr[0:1, 0:512],
                         start=False, stop=False),
                     wait=[bias_d] if t16 == 0 else [], inc=sP)
                b.op("tensor",
                     lambda e, bank=bank: e.matmul(
                         bank[:, :], ones_row_bf[0:1, 0:128], bvr[0:1, 512:1024],
                         start=False, stop=True),
                     inc=sP)
                mm_d = sP.at()
                b.op("vector",
                     lambda e, t16=t16, bank=bank: e.tensor_copy(
                         vtm[:, t16 * 520:(t16 + 1) * 520].rearrange(
                             "p (h x) -> p h x", x=65)[:, :, 0:64],
                         bank[:, :].rearrange("p (h d) -> p h d", d=64)),
                     wait=[mm_d, ones_d], inc=sV)
                vb_bank_free[t16 % 2] = sV.at()
            st["vtm_ready"] = (sV.at(), sV.at())

        def emit_qk_w(p):
            for wi, coff in ((0, p * 128), (1, D + p * 128)):
                for hf in range(2):
                    wt = [st["w_stg_free"]] if st["w_stg_free"] else []
                    b.dma("sync", wstg[:].rearrange("p (c n) -> p c n", n=128),
                          wat[4 * hf:4 * hf + 4, :, coff:coff + 128].rearrange("c p n -> p c n"),
                          wait=wt, inc=sL)
                    cw = [sL.at()]
                    if wi == 0 and hf == 0 and st["wbf_free"]:
                        cw.append(st["wbf_free"])
                    b.op("scalar",
                         lambda e, wi=wi, hf=hf: e.activation(
                             out=wbf[:, wi * 1024 + hf * 512:wi * 1024 + (hf + 1) * 512],
                             in_=wstg[:], func=AF.Copy),
                         wait=cw, inc=sA)
                    st["w_stg_free"] = sA.at()
            st["wcast"] = sA.at()
            bw0 = [st["bq_free"]] if st["bq_free"] else []
            b.dma("sync", bq2[:, 0:1], W["b_attn"][p * 128:(p + 1) * 128][:, None],
                  wait=bw0, inc=sL)
            b.dma("sync", bq2[:, 1:2],
                  W["b_attn"][D + p * 128:D + (p + 1) * 128][:, None], inc=sL)
            st["bias_d"] = sL.at()

        def emit_qk_main(p):
            wcast = st["wcast"]
            bias_d = st["bias_d"]
            qkb_free = [None, None]
            evd = {}
            for wi, (da, db_) in ((0, (qa, qb)), (1, (ka, kb))):
                for t in range(NT):
                    bank = pbk[(wi * NT + t) % 2]
                    wt0 = [wcast, bias_d]
                    if qkb_free[(wi * NT + t) % 2]:
                        wt0.append(qkb_free[(wi * NT + t) % 2])
                    if st["qk_banks"]:
                        wt0.append(st["qk_banks"])
                    for c in range(NCHUNK):
                        b.op("tensor",
                             lambda e, wi=wi, c=c, t=t, bank=bank: e.matmul(
                                 bank[:, :], wbf[:, wi * 1024 + c * 128:wi * 1024 + (c + 1) * 128],
                                 lnx[c][:, t * 512:(t + 1) * 512],
                                 start=(c == 0), stop=(c == NCHUNK - 1)),
                             wait=wt0 if c == 0 else [], inc=sP)
                    mm_d = sP.at()
                    ew = [mm_d, bias_d]
                    if st["qa_free"]:
                        ew.append(st["qa_free"])
                    b.op("scalar",
                         lambda e, da=da, t=t, wi=wi, bank=bank: e.activation(
                             out=da[0:64, t * 512:(t + 1) * 512], in_=bank[0:64, :],
                             func=AF.Identity, bias=bq2[0:64, wi:wi + 1]),
                         wait=ew, inc=sA)
                    b.op("scalar",
                         lambda e, db_=db_, t=t, wi=wi, bank=bank: e.activation(
                             out=db_[0:64, t * 512:(t + 1) * 512], in_=bank[64:128, :],
                             func=AF.Identity, bias=bq2[64:128, wi:wi + 1]),
                         inc=sA)
                    qkb_free[(wi * NT + t) % 2] = sA.at()
                    evd[(wi, t)] = sA.at()
            qk_mm_done = sP.at()
            st["bq_free"] = sA.at()
            st["wbf_free"] = qk_mm_done
            mm_done = {}
            sqrt_done = {}
            items = [(wi, hh, t) for wi in (0, 1) for hh in (0, 1) for t in range(NT)]
            for i, (wi, hh, t) in enumerate(items):
                src_ = (qa, qb, ka, kb)[wi * 2 + hh]
                slot = sqb[0:64, (i % 2) * 512:(i % 2) * 512 + 512]
                sw = [evd[(wi, t)]]
                if i >= 2:
                    sw.append(mm_done[i - 2])
                b.op("vector",
                     lambda e, src_=src_, t=t, slot=slot: e.tensor_mul(
                         slot, src_[0:64, t * 512:(t + 1) * 512],
                         src_[0:64, t * 512:(t + 1) * 512]),
                     wait=sw, inc=sV)
                bank = pbk[4 + i % 2]
                bw = [(sV, sV.n)]
                if i < 2:
                    if st["bc_free"][i % 2]:
                        bw.append(st["bc_free"][i % 2])
                if i >= 2:
                    bw.append(sqrt_done[i - 2])
                b.op("tensor",
                     lambda e, slot=slot, bank=bank: e.matmul(
                         bank[0:1, 0:512], ones64_bf[:, :], slot,
                         start=True, stop=True),
                     wait=bw, inc=sP)
                mm_done[i] = sP.at()
                rmap = {(0, 0): 0, (0, 1): 32, (1, 0): 64, (1, 1): 96}[(wi, hh)]
                b.op("scalar",
                     lambda e, rmap=rmap, t=t, bank=bank: e.activation(
                         out=rows[rmap:rmap + 1, t * 512:(t + 1) * 512],
                         in_=bank[0:1, 0:512], func=AF.Sqrt),
                     wait=[(sP, sP.n)], inc=sA)
                sqrt_done[i] = sA.at()
            st["b4_free"] = sA.at()
            rows_d = sA.at()
            for hh, qdst in ((0, qa), (1, qb)):
                b.op("vector",
                     lambda e, hh=hh: e.reduce_max(
                         rws[0:1, 4 * hh:4 * hh + 1],
                         rows[64 + 32 * hh:65 + 32 * hh, 0:T], axis=AX.X),
                     wait=[rows_d], inc=sV)
                b.op("vector",
                     lambda e, hh=hh: e.tensor_scalar_mul(
                         rws[0:1, 4 * hh + 1:4 * hh + 2], rws[0:1, 4 * hh:4 * hh + 1], -1.0),
                     inc=sV)
                b.op("vector",
                     lambda e, hh=hh: e.tensor_single_scalar(
                         rows2[32 * hh:32 * hh + 1, 0:T], rows[32 * hh:32 * hh + 1, 0:T],
                         rws[0:1, 4 * hh + 1:4 * hh + 2], op=ALU.mult),
                     inc=sV)
                aw = [(sV, sV.n)]
                if st["qa_free"]:
                    aw.append(st["qa_free"])
                if st.get("aug_prev"):
                    aw.append(st["aug_prev"])
                b.dma("sync", qdst[64:65, 0:T], rows2[32 * hh:32 * hh + 1, 0:T],
                      wait=aw, inc=sL)
            st["aug_prev"] = (sL, sL.n)
            st["aug_ld"] = sL.at()

        def emit_att(p):
            tiles = []
            for hh in (0, 1):
                for qg in range(4):
                    tl = [(t, 0, 512) for t in range(4 * qg)]
                    tl += [(4 * qg + j, j * 128, 512 - j * 128) for j in range(4)]
                    gidx = hh * 4 + qg
                    for ti, (kt, qoff, w) in enumerate(tl):
                        tiles.append((hh, qg, gidx, kt, qoff, w,
                                      ti == 0, ti == len(tl) - 1))
            expd = {}
            avdn = {}
            pend = st.setdefault("pend", [])
            pav = (sP, sP.n)
            tick0 = st.setdefault("ticker", 0)

            def emit_epiB(gidx, hh, qg, p_, hilo):
                bcb = pbk[4 + gidx % 2]
                av = pbk[2 + gidx % 2]
                yoff = 64 * hh
                bw = [hilo]
                if st["bc_free"][gidx % 2]:
                    bw.append(st["bc_free"][gidx % 2])
                if st["b4_free"]:
                    bw.append(st["b4_free"])
                    st["b4_free"] = None
                b.op("tensor",
                     lambda e, bcb=bcb: e.matmul(
                         bcb[0:64, 0:512], ones_row_bf[0:1, 0:64], hlr[0:1, 0:512],
                         start=True, stop=False),
                     wait=bw, inc=sP)
                b.op("tensor",
                     lambda e, bcb=bcb: e.matmul(
                         bcb[0:64, 0:512], ones_row_bf[0:1, 0:64], hlr[0:1, 512:1024],
                         start=False, stop=True),
                     inc=sP)
                dw = [(sP, sP.n)]
                if st["dbc_free"][gidx % 2]:
                    dw.append(st["dbc_free"][gidx % 2])
                b.op("scalar",
                     lambda e, gidx=gidx, bcb=bcb: e.activation(
                         out=dbc[gidx % 2][:, :], in_=bcb[0:64, 0:512], func=AF.Copy),
                     wait=dw, inc=sA)
                st["bc_free"][gidx % 2] = sA.at()
                yw = [(sA, sA.n)]
                if st["ybf_free"]:
                    yw.append(st["ybf_free"])
                av = pbk[2 + gidx % 2]
                b.op("vector",
                     lambda e, p=p_, yoff=yoff, qg=qg, av=av, gidx=gidx: e.tensor_mul(
                         ybf[p % 4][yoff:yoff + 64, qg * 512:(qg + 1) * 512],
                         av[0:64, 0:512], dbc[gidx % 2][:, :]),
                     wait=yw, inc=sV)
                st["av_free"][gidx % 2] = sV.at()
                st["dbc_free"][gidx % 2] = sV.at()

            n = len(tiles)
            for i in range(n + 2):
                st["ticker"] = tick0 + i
                while pend and pend[0][0] <= tick0 + i:
                    ent = pend.pop(0)
                    emit_epiB(*ent[1:])
                if i < n:
                    hh, qg, gidx, kt, qoff, w, first, last = tiles[i]
                    q_t = qa if hh == 0 else qb
                    k_t = ka if hh == 0 else kb
                    hloc = (p % 4) * 2 + hh
                    scb = pbk[i % 2]
                    wt = []
                    if i >= 2:
                        wt.append(expd[i - 2])
                    if i < 2:
                        wt += [(sA, sA.n), st["aug_ld"],
                               st["vtm_ready"][0], st["vtm_ready"][1]]
                        if st["qk_banks"]:
                            wt.append(st["qk_banks"])
                    b.op("tensor",
                         lambda e, kt=kt, qoff=qoff, w=w, scb=scb, q_t=q_t, k_t=k_t, qg=qg: e.matmul(
                             scb[0:128, 0:w],
                             k_t[0:65, kt * 128:(kt + 1) * 128],
                             q_t[0:65, qg * 512 + qoff:(qg + 1) * 512],
                             start=True, stop=True),
                         wait=wt, inc=sP)
                    scd = sP.at()
                    diag = kt >= 4 * qg
                    if diag:
                        b.op("vector",
                             lambda e, scb=scb: e.tensor_add(
                                 scb[0:128, 0:128], scb[0:128, 0:128], tri_sb[:]),
                             wait=[scd], inc=sV)
                    ew = [scd] + ([(sV, sV.n)] if diag else [])
                    if i >= 4:
                        ew.append(avdn[i - 4])
                    else:
                        ew.append(pav)
                    b.op("scalar",
                         lambda e, i=i, w=w, scb=scb: e.activation(
                             out=ebuf[:, (i % 4) * 512:(i % 4) * 512 + w],
                             in_=scb[0:128, 0:w], func=AF.Exp, scale=0.125),
                         wait=ew, inc=sA)
                    expd[i] = sA.at()
                j = i - 2
                if 0 <= j < n:
                    hh, qg, gidx, kt, qoff, w, first, last = tiles[j]
                    hloc = (p % 4) * 2 + hh
                    av = pbk[2 + gidx % 2]
                    wt = [expd[j]]
                    if first and st["av_free"][gidx % 2]:
                        wt.append(st["av_free"][gidx % 2])
                    b.op("tensor",
                         lambda e, kt=kt, qoff=qoff, w=w, j=j, av=av, hloc=hloc,
                                first=first, last=last: e.matmul(
                             av[0:65, qoff:512],
                             vtm[:, kt * 520 + hloc * 65:kt * 520 + hloc * 65 + 65],
                             ebuf[:, (j % 4) * 512:(j % 4) * 512 + w],
                             start=first, stop=last),
                         wait=wt, inc=sP)
                    avdn[j] = sP.at()
                    if last:
                        avd = sP.at()
                        b.op("vector",
                             lambda e, av=av: e.reciprocal(rws[0:1, 8:520],
                                                           av[64:65, 0:512]),
                             wait=[avd], inc=sV)
                        b.op("scalar",
                             lambda e: e.activation(out=hlr[0:1, 0:512],
                                                    in_=rws[0:1, 8:520], func=AF.Copy),
                             wait=[(sV, sV.n)], inc=sA)
                        b.op("vector",
                             lambda e: e.tensor_sub(rws[0:1, 528:1040], rws[0:1, 8:520],
                                                    hlr[0:1, 0:512]),
                             wait=[(sA, sA.n)], inc=sV)
                        b.op("scalar",
                             lambda e: e.activation(out=hlr[0:1, 512:1024],
                                                    in_=rws[0:1, 528:1040], func=AF.Copy),
                             wait=[(sV, sV.n)], inc=sA)
                        pend.append((tick0 + i + 2, gidx, hh, qg, p, sA.at()))
            st["ticker"] = tick0 + n + 2
            st["epiB"] = emit_epiB
            st["qa_free"] = sP.at()
            st["qk_banks"] = sA.at()
            st["vtm_free"] = sP.at()

        def emit_proj(r):
            import os as _os
            _pc = int(_os.environ.get("PROJCUT", "3"))
            for ent in st.get("pend", []):
                st["epiB"](*ent[1:])
            st["pend"] = []
            ymuls = sV.at()
            for oc in range(NCHUNK):
                wt = [st["pstg_free"]] if st["pstg_free"] else []
                if oc == 0 and st["wv_stg_free"]:
                    wt = wt + [st["wv_stg_free"]]
                b.dma("sync", pstg[:].rearrange("p (c n) -> p c n", n=128),
                      wpt[4 * r:4 * r + 4, :, oc * 128:(oc + 1) * 128].rearrange("c p n -> p c n"),
                      wait=wt, inc=sL)
                dst = oc % 2
                cw = [sL.at()]
                if oc == 0:
                    cw.append((sP, sP.n))
                if st["pbf_free"][dst]:
                    cw.append(st["pbf_free"][dst])
                b.op("scalar",
                     lambda e, dst=dst: e.activation(out=pbf[dst][:], in_=pstg[:],
                                                     func=AF.Copy),
                     wait=cw, inc=sA)
                st["pstg_free"] = sA.at()
                castd = sA.at()
                for t in range(NT):
                    if _pc < 2:
                        break
                    bank = pbk[t % 2]
                    wt0 = [castd]
                    if st["pj_bfree"][t % 2]:
                        wt0.append(st["pj_bfree"][t % 2])
                    if oc == 0 and t == 0:
                        wt0 += [ymuls, (sA, sA.n)]
                        if st["qk_banks"]:
                            wt0.append(st["qk_banks"])
                    for cc in range(4):
                        b.op("tensor",
                             lambda e, cc=cc, t=t, bank=bank, dst=dst: e.matmul(
                                 bank[:, :], pbf[dst][:, cc * 128:(cc + 1) * 128],
                                 ybf[cc][:, t * 512:(t + 1) * 512],
                                 start=(cc == 0), stop=(cc == 3)),
                             wait=wt0 if cc == 0 else [], inc=sP)
                    if _pc < 3:
                        continue
                    if _os.environ.get("PROJB") == "0":
                        b.op("vector",
                             lambda e, oc=oc, t=t, bank=bank: e.scalar_tensor_tensor(
                                 out=stream[oc][:, t * 512:(t + 1) * 512],
                                 in0=bank[:, :], scalar=1.0,
                                 in1=stream[oc][:, t * 512:(t + 1) * 512],
                                 op0=ALU.mult, op1=ALU.add),
                             wait=[(sP, sP.n)], inc=sV)
                    elif r == 0:
                        b.op("vector",
                             lambda e, oc=oc, t=t, bank=bank: e.scalar_tensor_tensor(
                                 out=stream[oc][:, t * 512:(t + 1) * 512],
                                 in0=bank[:, :], scalar=bproj_sb[:, oc:oc + 1],
                                 in1=stream[oc][:, t * 512:(t + 1) * 512],
                                 op0=ALU.add, op1=ALU.add),
                             wait=[(sP, sP.n)], inc=sV)
                    else:
                        b.op("vector",
                             lambda e, oc=oc, t=t, bank=bank: e.scalar_tensor_tensor(
                                 out=stream[oc][:, t * 512:(t + 1) * 512],
                                 in0=bank[:, :], scalar=1.0,
                                 in1=stream[oc][:, t * 512:(t + 1) * 512],
                                 op0=ALU.mult, op1=ALU.add),
                             wait=[(sP, sP.n)], inc=sV)
                    st["pj_bfree"][t % 2] = sV.at()
                st["pbf_free"][dst] = sP.at()
            st["pj_bfree"] = [None, None]
            st["ybf_free"] = sP.at()
            st["qk_banks"] = sV.at()
            st["x_free"] = sA.at()

        steps = []
        for r in range(2):
            steps.append(lambda r=r: emit_vb(r))
            steps.append(lambda r=r: emit_qk_w(4 * r))
            steps.append(lambda r=r: emit_qk_main(4 * r))
            for p in range(4 * r, 4 * r + 4):
                if p % 4 != 3:
                    steps.append(lambda p=p: emit_qk_w(p + 1))
                steps.append(lambda p=p: emit_att(p))
                if p % 4 != 3:
                    steps.append(lambda p=p: emit_qk_main(p + 1))
            steps.append(lambda r=r: emit_proj(r))
        import os as _os
        for _i, s_ in enumerate(steps):
            if _i < int(_os.environ.get("L0CUT", "99")):
                s_()
        bar()
        if _os.environ.get("ATTDBG") == "1":
            taps = (("d_qa", qa), ("d_qb", qb), ("d_ka", ka), ("d_kb", kb),
                    ("d_rows", rows), ("d_rows2", rows2), ("d_ybf0", ybf[0]),
                    ("d_ebuf", ebuf), ("d_hlr", hlr), ("d_wbf", wbf))
            for nm, sb in taps:
                hh_ = nc.dram_tensor(nm, list(sb.shape), sb.dtype,
                                     kind="ExternalOutput").ap()
                b.dma("gpsimd", hh_, sb[:], inc=sG)
            hh_ = nc.dram_tensor("d_vtm", [128, 16 * 520], BF16,
                                 kind="ExternalOutput").ap()
            b.dma("gpsimd", hh_, vtm[:], inc=sG)
            hh_ = nc.dram_tensor("d_rws", [1, 1048], F32,
                                 kind="ExternalOutput").ap()
            b.dma("gpsimd", hh_, rws[:], inc=sG)
            hh_ = nc.dram_tensor("d_dbc0", [64, 512], F32,
                                 kind="ExternalOutput").ap()
            b.dma("gpsimd", hh_, dbc[0][:], inc=sG)
            bar()

    def emit_moe():
        pool = Pool()
        hid = [pool.t(f"m_hid{i}", [128, T], BF16) for i in range(8)]
        wdh = [pool.t(f"m_wdh{i}", [128, D], BF16) for i in range(8)]
        wgc = [pool.t(f"m_wgc{i}", [128, D], BF16) for i in range(2)]
        wuc = [pool.t(f"m_wuc{i}", [128, D], BF16) for i in range(2)]
        stg_g = pool.t("m_sg", [128, D], F32)
        stg_u = pool.t("m_su", [128, D], F32)
        stg_d = pool.t("m_sd", [128, D], F32)
        wbc = pool.t("m_wbc", [128, T], BF16)
        silu_g = pool.t("m_silu", [128, T], BF16)
        gwall = pool.t("m_gwall", [128, 64], F32)
        g1row = pool.t("m_g1", [1, 8], F32)
        rcol = pool.t("m_rcol", [128, 1], F32)
        mgrow = pool.t("m_mgrow", [128, 8], F32)
        routs = pool.t("m_routs", [128, 64], F32)
        wpeT = pool.t("m_wpeT", [8, T], BF16)
        wpe_bf = pool.t("m_wpebf", [128, 8], BF16)

        gwt = _chunks(W["gate_w"])
        for c in range(NCHUNK):
            b.dma("sync", gwall[:, c * 8:(c + 1) * 8], gwt[c], inc=sL)
        gwld = sL.at()
        for c in range(NCHUNK):
            b.op("tensor",
                 lambda e, c=c: e.matmul(pbk[7][0:1, 0:8], ones_col_f[:],
                                         gwall[:, c * 8:(c + 1) * 8],
                                         start=(c == 0), stop=(c == NCHUNK - 1)),
                 wait=[gwld] if c == 0 else [], inc=sP)
        b.op("scalar",
             lambda e: e.activation(out=g1row[:], in_=pbk[7][0:1, 0:8], func=AF.Copy),
             wait=[(sP, sP.n)], inc=sA)
        g1d = sA.at()

        P_ = routs[:, 0:8]
        TOP = routs[:, 8:16]
        IS1 = routs[:, 16:24]
        IS2 = routs[:, 24:32]
        WPE = routs[:, 32:40]
        LG = routs[:, 40:48]
        SC = routs[:, 48:56]
        for t16 in range(NT128):
            for c in range(NCHUNK):
                b.op("tensor",
                     lambda e, c=c, t16=t16: e.matmul(
                         pbk[0][0:128, 0:8],
                         stream[c][:, t16 * 128:(t16 + 1) * 128],
                         gwall[:, c * 8:(c + 1) * 8],
                         start=(c == 0), stop=(c == NCHUNK - 1)),
                     wait=[gwld, (sV, sV.n)] if c == 0 else [], inc=sP)
            b.op("tensor",
                 lambda e, t16=t16: e.matmul(
                     pbk[1][0:128, 0:8],
                     statr[0:1, t16 * 128:(t16 + 1) * 128], g1row[:],
                     start=True, stop=True),
                 wait=[g1d, (sA, sA.n)], inc=sP)
            b.op("tensor",
                 lambda e, t16=t16: e.matmul(
                     pbk[2][0:128, 0:1],
                     statr[64:65, t16 * 128:(t16 + 1) * 128], ones65_f[64:65, 0:1],
                     start=True, stop=True),
                 inc=sP)
            mmd = sP.at()
            b.op("scalar",
                 lambda e: e.activation(out=mgrow[:], in_=pbk[1][0:128, 0:8], func=AF.Copy),
                 wait=[mmd], inc=sA)
            b.op("scalar",
                 lambda e: e.activation(out=rcol[:], in_=pbk[2][0:128, 0:1], func=AF.Copy),
                 inc=sA)
            evd = sA.at()
            b.op("vector",
                 lambda e: e.tensor_sub(LG, pbk[0][0:128, 0:8], mgrow[:]),
                 wait=[mmd, evd], inc=sV)
            b.op("vector", lambda e: e.tensor_scalar_mul(LG, LG, rcol[:, 0:1]), inc=sV)
            b.op("vector", lambda e: e.reduce_max(SC[:, 0:1], LG, axis=AX.X),
                 wait=[sV.at()], inc=sV)
            b.op("vector",
                 lambda e: e.tensor_scalar_mul(SC[:, 1:2], SC[:, 0:1], -1.0),
                 wait=[sV.at()], inc=sV)
            vd = sV.at()
            b.op("scalar",
                 lambda e: e.activation(out=P_, in_=LG, func=AF.Exp,
                                        bias=SC[:, 1:2], accum_out=SC[:, 2:3]),
                 wait=[vd], inc=sA)
            b.op("vector", lambda e: e.reciprocal(SC[:, 3:4], SC[:, 2:3]),
                 wait=[sA.at()], inc=sV)
            b.op("vector", lambda e: e.tensor_scalar_mul(P_, P_, SC[:, 3:4]),
                 wait=[sV.at()], inc=sV)
            b.op("vector", lambda e: e.max(out=TOP, in_=P_),
                 wait=[sV.at()], inc=sV)
            topd = sV.at()
            b.op("vector",
                 lambda e: e.tensor_single_scalar(IS1, P_, TOP[:, 0:1], op=ALU.is_equal),
                 wait=[topd], inc=sV)
            b.op("vector",
                 lambda e: e.tensor_single_scalar(IS2, P_, TOP[:, 1:2], op=ALU.is_equal),
                 inc=sV)
            b.op("vector",
                 lambda e: e.tensor_add(SC[:, 4:5], TOP[:, 0:1], TOP[:, 1:2]), inc=sV)
            b.op("vector",
                 lambda e: e.tensor_scalar_add(SC[:, 4:5], SC[:, 4:5], 1e-20), inc=sV)
            b.op("vector", lambda e: e.reciprocal(SC[:, 5:6], SC[:, 4:5]),
                 wait=[sV.at()], inc=sV)
            b.op("vector", lambda e: e.tensor_add(WPE, IS1, IS2),
                 wait=[sV.at()], inc=sV)
            b.op("vector", lambda e: e.tensor_mul(WPE, WPE, P_), inc=sV)
            b.op("vector", lambda e: e.tensor_scalar_mul(WPE, WPE, SC[:, 5:6]), inc=sV)
            b.op("scalar",
                 lambda e: e.activation(out=wpe_bf[:], in_=WPE, func=AF.Copy),
                 wait=[(sV, sV.n)], inc=sA)
            b.op("tensor",
                 lambda e: e.matmul(pbk[3][0:8, 0:128], wpe_bf[:], iden_bf[:],
                                    start=True, stop=True),
                 wait=[(sA, sA.n)], inc=sP)
            b.op("scalar",
                 lambda e, t16=t16: e.activation(
                     out=wpeT[:, t16 * 128:(t16 + 1) * 128],
                     in_=pbk[3][0:8, 0:128], func=AF.Copy),
                 wait=[(sP, sP.n)], inc=sA)
        bar()

        mstate = {"stg": None, "dstg": None}
        bfree = [None] * 8
        silu_free = [None] * NT
        wbuf_free = [None, None]
        wdh_free = None
        for ex in range(9):
            shared = (ex == 8)
            if not shared:
                wg_t = _chunks(W["w_gate"][ex])
                wu_t = _chunks(W["w_up"][ex])
                wd_t = _chunks(W["w_down"][ex])
            else:
                wg_t = _chunks(W["ws_gate"])
                wu_t = _chunks(W["ws_up"])
                wd_t = _chunks(W["ws_down"])

            wbc_done = None
            if not shared:
                for t in range(NT):
                    wt = [bfree[7]] if bfree[7] else []
                    b.op("tensor",
                         lambda e, ex=ex, t=t: e.matmul(
                             pbk[7][0:128, :], sel_sb[:, ex * 128:(ex + 1) * 128],
                             wpeT[:, t * 512:(t + 1) * 512],
                             start=True, stop=True),
                         wait=wt, inc=sP)
                    b.op("scalar",
                         lambda e, t=t: e.activation(
                             out=wbc[:, t * 512:(t + 1) * 512],
                             in_=pbk[7][0:128, :], func=AF.Copy),
                         wait=[(sP, sP.n), (sV, sV.n)], inc=sA)
                    bfree[7] = (sA, sA.n)
                wbc_done = sA.at()

            for half in range(2):
                for dc8 in range(8):
                    dc = half * 8 + dc8
                    buf = dc8 % 2
                    wt = [mstate["stg"]] if mstate["stg"] else []
                    b.dma("sync",
                          stg_g[:].rearrange("p (c n) -> p c n", n=128),
                          wg_t[:, :, dc * 128:(dc + 1) * 128].rearrange("c p n -> p c n"),
                          wait=wt, inc=sL)
                    b.dma("sync",
                          stg_u[:].rearrange("p (c n) -> p c n", n=128),
                          wu_t[:, :, dc * 128:(dc + 1) * 128].rearrange("c p n -> p c n"),
                          inc=sL)
                    ld = sL.at()
                    cwt = [ld] + ([wbuf_free[buf]] if wbuf_free[buf] else [])
                    b.op("scalar",
                         lambda e, buf=buf: e.activation(out=wgc[buf][:],
                                                         in_=stg_g[:], func=AF.Copy),
                         wait=cwt, inc=sA)
                    b.op("scalar",
                         lambda e, buf=buf: e.activation(out=wuc[buf][:],
                                                         in_=stg_u[:], func=AF.Copy),
                         inc=sA)
                    mstate["stg"] = sA.at()
                    castd = sA.at()
                    gstop = {}
                    for c in range(NCHUNK):
                        for t in range(NT):
                            wt = []
                            if c == 0:
                                wt = [bfree[t]] if bfree[t] else []
                                if t == 0:
                                    wt = wt + [castd]
                            b.op("tensor",
                                 lambda e, c=c, t=t, buf=buf: e.matmul(
                                     pbk[t][:, :], wgc[buf][:, c * 128:(c + 1) * 128],
                                     lnx[c][:, t * 512:(t + 1) * 512],
                                     start=(c == 0), stop=(c == NCHUNK - 1)),
                                 wait=wt, inc=sP)
                            if c == NCHUNK - 1:
                                gstop[t] = (sP, sP.n)
                    for t in range(NT):
                        wt = [gstop[t]] + ([silu_free[t]] if silu_free[t] else [])
                        b.op("scalar",
                             lambda e, t=t: e.activation(
                                 out=silu_g[:, t * 512:(t + 1) * 512],
                                 in_=pbk[t][:, :], func=AF.Silu),
                             wait=wt, inc=sA)
                        bfree[t] = (sA, sA.n)
                    silud = {t: bfree[t] for t in range(NT)}
                    ustop = {}
                    for c in range(NCHUNK):
                        for t in range(NT):
                            wt = []
                            if c == 0:
                                wt = [bfree[4 + t]] if bfree[4 + t] else []
                            b.op("tensor",
                                 lambda e, c=c, t=t, buf=buf: e.matmul(
                                     pbk[4 + t][:, :], wuc[buf][:, c * 128:(c + 1) * 128],
                                     lnx[c][:, t * 512:(t + 1) * 512],
                                     start=(c == 0), stop=(c == NCHUNK - 1)),
                                 wait=wt, inc=sP)
                            if c == NCHUNK - 1:
                                ustop[t] = (sP, sP.n)
                    wbuf_free[buf] = (sP, sP.n)
                    for t in range(NT):
                        if shared:
                            b.op("vector",
                                 lambda e, t=t, dc8=dc8: e.tensor_mul(
                                     hid[dc8][:, t * 512:(t + 1) * 512],
                                     pbk[4 + t][:, :],
                                     silu_g[:, t * 512:(t + 1) * 512]),
                                 wait=[ustop[t], silud[t]], inc=sV)
                        else:
                            b.op("vector",
                                 lambda e, t=t, dc8=dc8: e.tensor_mul(
                                     hid[dc8][:, t * 512:(t + 1) * 512],
                                     pbk[4 + t][:, :],
                                     wbc[:, t * 512:(t + 1) * 512]),
                                 wait=[ustop[t], silud[t], wbc_done], inc=sV)
                            b.op("vector",
                                 lambda e, t=t, dc8=dc8: e.tensor_mul(
                                     hid[dc8][:, t * 512:(t + 1) * 512],
                                     hid[dc8][:, t * 512:(t + 1) * 512],
                                     silu_g[:, t * 512:(t + 1) * 512]),
                                 inc=sV)
                        bfree[4 + t] = (sV, sV.n)
                        silu_free[t] = (sV, sV.n)
                hidd = sV.at()

                for dc8 in range(8):
                    dc = half * 8 + dc8
                    wt = [mstate["dstg"]] if mstate["dstg"] else []
                    b.dma("gpsimd", stg_d[:], wd_t[dc], wait=wt, inc=sG)
                    cwt = [sG.at()] + ([wdh_free] if wdh_free else [])
                    b.op("scalar",
                         lambda e, dc8=dc8: e.activation(out=wdh[dc8][:],
                                                         in_=stg_d[:], func=AF.Copy),
                         wait=cwt, inc=sA)
                    mstate["dstg"] = sA.at()
                wdc = sA.at()

                for oc in range(NCHUNK):
                    for t in range(NT):
                        bi = t % 4
                        bank = pbk[bi]
                        for dc8 in range(8):
                            wt = []
                            if dc8 == 0:
                                wt = [bfree[bi]] if bfree[bi] else []
                                if oc == 0 and t == 0:
                                    wt = wt + [wdc, hidd]
                            b.op("tensor",
                                 lambda e, oc=oc, t=t, dc8=dc8, bank=bank: e.matmul(
                                     bank[:, :], wdh[dc8][:, oc * 128:(oc + 1) * 128],
                                     hid[dc8][:, t * 512:(t + 1) * 512],
                                     start=(dc8 == 0), stop=(dc8 == 7)),
                                 wait=wt, inc=sP)
                        dstop = (sP, sP.n)
                        b.op("vector",
                             lambda e, oc=oc, t=t, bank=bank: e.scalar_tensor_tensor(
                                 out=stream[oc][:, t * 512:(t + 1) * 512],
                                 in0=bank[:, :], scalar=1.0,
                                 in1=stream[oc][:, t * 512:(t + 1) * 512],
                                 op0=ALU.mult, op1=ALU.add),
                             wait=[dstop], inc=sV)
                        bfree[bi] = (sV, sV.n)
                wdh_free = (sP, sP.n)
        bar()

    def emit_l1_attn():
        pool = Pool()
        kfm = [pool.t(f"b_k{c}", [128, T], BF16) for c in range(NCHUNK)]
        vtm1 = pool.t("b_vtm", [128, 16 * 1024], BF16)
        rows1 = pool.t("b_rows", [64, T], BF16)
        rws1 = pool.t("b_rws", [1, 1024], F32)
        sq1 = pool.t("b_sq", [128, 2 * 512], BF16)
        cb = pool.t("b_cb", [128, 1], F32)
        tmps = pool.t("b_tmps", [128, 256], F32)
        avsb = pool.t("b_avsb", [128, 256], F32)
        hlr1 = pool.t("b_hlr", [1, 512], BF16)
        wstg = pool.t("b_wstg", [128, 1024], F32)
        wstg2 = pool.t("b_wstg2", [128, 512], F32)
        wbf = [pool.t(f"b_wbf{i}", [128, 1024], BF16) for i in range(2)]
        pX = pool.fork()
        wvbf_a = pX.t("b_wvbf", [128, 8 * 256], BF16)
        wvbf_b = pX.t("b_wvbf2", [128, 8 * 256], BF16)
        pY2 = pool.fork()
        qfm = pY2.t("b_qfm", [128, 8 * 256], BF16)
        ET = pY2.t("b_et", [128, 3 * 256], BF16)
        dbc1 = [pY2.t("b_dbc0", [128, 256], F32)]

        import os as _os
        _lc = int(_os.environ.get("L1CUT", "99"))
        st = {"stg": None, "buf": [None, None], "x_free": None}

        def stream_w(wap, oc, buf, eng="scalar"):
            wt = [st["stg"]] if st["stg"] else []
            b.dma("sync",
                  wstg[:].rearrange("p (c n) -> p c n", n=128),
                  _chunks(wap)[:, :, oc * 128:(oc + 1) * 128].rearrange("c p n -> p c n"),
                  wait=wt, inc=sL)
            cwt = [sL.at()] + ([st["buf"][buf]] if st["buf"][buf] else [])
            if eng == "scalar":
                b.op("scalar",
                     lambda e, buf=buf: e.activation(out=wbf[buf][:], in_=wstg[:],
                                                     func=AF.Copy),
                     wait=cwt, inc=sA)
                st["stg"] = sA.at()
                return sA.at()
            b.op("vector",
                 lambda e, buf=buf: e.tensor_copy(wbf[buf][:], wstg[:]),
                 wait=cwt, inc=sV)
            st["stg"] = sV.at()
            return sV.at()

        kv_bfree = [None, None]
        kev = {}
        for oc in range(NCHUNK if _lc >= 1 else 0):
            buf = oc % 2
            castd = stream_w(W["wk"], oc, buf)
            for t in range(NT):
                bank = pbk[t % 2]
                for c in range(NCHUNK):
                    wt = []
                    if c == 0:
                        wt = ([kv_bfree[t % 2]] if kv_bfree[t % 2] else []) + [castd]
                    b.op("tensor",
                         lambda e, c=c, t=t, buf=buf, bank=bank: e.matmul(
                             bank[:, :], wbf[buf][:, c * 128:(c + 1) * 128],
                             lnx[c][:, t * 512:(t + 1) * 512],
                             start=(c == 0), stop=(c == NCHUNK - 1)),
                         wait=wt, inc=sP)
                b.op("scalar",
                     lambda e, oc=oc, t=t, bank=bank: e.activation(
                         out=kfm[oc][:, t * 512:(t + 1) * 512], in_=bank[:, :],
                         func=AF.Copy),
                     wait=[(sP, sP.n)], inc=sA)
                kv_bfree[t % 2] = (sA, sA.n)
                kev[(oc, t)] = sA.at()
            st["buf"][buf] = (sP, sP.n)
        mmk = {}
        for i, (t, oc) in enumerate([(t, oc) for t in range(NT) for oc in range(NCHUNK)]
                                    if _lc >= 1 else []):
            slot = sq1[:, (i % 2) * 512:(i % 2) * 512 + 512]
            sw = [kev[(oc, t)]]
            if i >= 2:
                sw.append(mmk[i - 2])
            b.op("vector",
                 lambda e, oc=oc, t=t, slot=slot: e.tensor_mul(
                     slot, kfm[oc][:, t * 512:(t + 1) * 512],
                     kfm[oc][:, t * 512:(t + 1) * 512]),
                 wait=sw, inc=sV)
            b.op("tensor",
                 lambda e, t=t, oc=oc, slot=slot: e.matmul(
                     pbk[4 + t][0:1, 0:512], onesc_bf[:, :], slot,
                     start=(oc == 0), stop=(oc == NCHUNK - 1)),
                 wait=[(sV, sV.n)], inc=sP)
            mmk[i] = sP.at()
        for t in range(NT if _lc >= 1 else 0):
            b.op("scalar",
                 lambda e, t=t: e.activation(
                     out=rows1[0:1, t * 512:(t + 1) * 512],
                     in_=pbk[4 + t][0:1, 0:512], func=AF.Sqrt),
                 wait=[(sP, sP.n)], inc=sA)
        if _lc >= 1:
            b.op("vector",
                 lambda e: e.reduce_max(rws1[0:1, 800:801], rows1[0:1, 0:T], axis=AX.X),
                 wait=[(sA, sA.n)], inc=sV)
        kmax_d = sV.at()

        castd_h = {}
        wv_read = {}

        def wv_load(q4):
            wvb = (wvbf_a, wvbf_b)[q4 % 2]
            cw0 = [wv_read[q4 - 2]] if (q4 - 2) in wv_read else []
            for qtr in range(4):
                wt = [st["wv_stg"]] if st.get("wv_stg") else []
                b.dma("sync", wstg2[:].rearrange("p (c n) -> p c n", n=64),
                      _chunks(W["wv"])[:, :, q4 * 256 + qtr * 64:
                                       q4 * 256 + (qtr + 1) * 64].rearrange("c p n -> p c n"),
                      wait=wt, inc=sL)
                b.op("scalar",
                     lambda e, qtr=qtr, wvb=wvb: e.activation(
                         out=wvb[:].rearrange("p (c n) -> p c n", n=256)[
                             :, :, qtr * 64:(qtr + 1) * 64],
                         in_=wstg2[:].rearrange("p (c n) -> p c n", n=64),
                         func=AF.Copy),
                     wait=(cw0 if qtr == 0 else []) + [sL.at()], inc=sA)
                st["wv_stg"] = sA.at()
            castd_h[q4] = sA.at()
        if _lc >= 2:
            wv_load(0)
            wv_load(1)
        vb_free = [None, None]
        for q4 in range(4 if _lc >= 2 else 0):
            wvb = (wvbf_a, wvbf_b)[q4 % 2]
            castd = castd_h[q4]
            for t16 in range(NT128):
                bank = pbk[2 + t16 % 2]
                wt0 = [castd]
                if q4 == 0 and t16 == 0:
                    wt0.append((sP, sP.n))
                if vb_free[t16 % 2]:
                    wt0.append(vb_free[t16 % 2])
                for c in range(NCHUNK):
                    b.op("tensor",
                         lambda e, c=c, t16=t16, bank=bank, wvb=wvb: e.matmul(
                             bank[0:128, 0:256], lnx[c][:, t16 * 128:(t16 + 1) * 128],
                             wvb[:, c * 256:(c + 1) * 256],
                             start=(c == 0), stop=(c == NCHUNK - 1)),
                         wait=wt0 if c == 0 else [], inc=sP)
                b.op("scalar",
                     lambda e, t16=t16, q4=q4, bank=bank: e.activation(
                         out=vtm1[:, t16 * 1024 + q4 * 256:t16 * 1024 + (q4 + 1) * 256],
                         in_=bank[0:128, 0:256], func=AF.Copy),
                     wait=[(sP, sP.n)], inc=sA)
                vb_free[t16 % 2] = (sA, sA.at()[1])
            wv_read[q4] = (sP, sP.n)
            if _lc >= 2 and q4 + 2 < 4:
                wv_load(q4 + 2)
        vtm_d = sA.at()
        wv_read_done = sP.at()

        l1s = {"av_free": None, "bc_free": None, "qfm_free": None,
               "cb_free": None, "dbc1_free": None, "pend": None}

        def emit_epi1(qg, hilo):
            bw = [hilo]
            if l1s["bc_free"]:
                bw.append(l1s["bc_free"])
            b.op("tensor",
                 lambda e: e.matmul(pbk[7][0:128, 0:256], onesr_bf[0:1, 0:128],
                                    hlr1[0:1, 0:256], start=True, stop=False),
                 wait=bw, inc=sP)
            b.op("tensor",
                 lambda e: e.matmul(pbk[7][0:128, 0:256], onesr_bf[0:1, 0:128],
                                    hlr1[0:1, 256:512], start=False, stop=True),
                 inc=sP)
            dw = [(sP, sP.n)]
            if l1s["dbc1_free"]:
                dw.append(l1s["dbc1_free"])
            b.op("scalar",
                 lambda e: e.activation(out=dbc1[0][:, :],
                                        in_=pbk[7][0:128, 0:256], func=AF.Copy),
                 wait=dw, inc=sA)
            l1s["bc_free"] = sA.at()
            l1s["cb_free"] = sA.at()
            for oc in range(NCHUNK):
                avb = pbk[2 + oc // 2]
                aoff = (oc % 2) * 256
                b.op("scalar",
                     lambda e, oc=oc, avb=avb, aoff=aoff: e.activation(
                         out=avsb[:, :], in_=avb[0:128, aoff:aoff + 256], func=AF.Copy),
                     wait=[(sA, sA.n), (sV, max(0, sV.n - 1))], inc=sA)
                b.op("vector",
                     lambda e, oc=oc, qg=qg: e.tensor_mul(
                         tmps[:, :], avsb[:, :], dbc1[0][:, :]),
                     wait=[(sA, sA.n)], inc=sV)
                b.op("vector",
                     lambda e, oc=oc, qg=qg: e.tensor_add(
                         stream[oc][:, qg * 256:(qg + 1) * 256],
                         stream[oc][:, qg * 256:(qg + 1) * 256], tmps[:, :]),
                     inc=sV)
            l1s["av_free"] = sV.at()
            l1s["dbc1_free"] = sV.at()

        for qg in range(min(8, max(0, _lc - 2))):
            for oc in range(NCHUNK):
                buf = oc % 2
                castd = stream_w(W["wq"], oc, buf, eng="vector")
                if qg == 0 and oc == 0:
                    castd_l = [castd, wv_read_done, (sA, sA.n)]
                else:
                    castd_l = [castd, (sA, sA.n)]
                bank = pbk[0] if oc % 2 == 0 else pbk[1]
                wt0 = list(castd_l)
                if l1s["qfm_free"]:
                    wt0.append(l1s["qfm_free"])
                for c in range(NCHUNK):
                    b.op("tensor",
                         lambda e, c=c, oc=oc, qg=qg, buf=buf, bank=bank: e.matmul(
                             bank[0:128, 0:256], wbf[buf][:, c * 128:(c + 1) * 128],
                             lnx[c][:, qg * 256:(qg + 1) * 256],
                             start=(c == 0), stop=(c == NCHUNK - 1)),
                         wait=wt0 if c == 0 else [], inc=sP)
                b.op("scalar",
                     lambda e, oc=oc, bank=bank: e.activation(
                         out=qfm[:, oc * 256:(oc + 1) * 256], in_=bank[0:128, 0:256],
                         func=AF.Copy),
                     wait=[(sP, sP.n)], inc=sA)
                st["buf"][buf] = (sP, sP.n)
                b.op("vector",
                     lambda e, oc=oc: e.tensor_mul(
                         sq1[:, 0:256], qfm[:, oc * 256:(oc + 1) * 256],
                         qfm[:, oc * 256:(oc + 1) * 256]),
                     wait=[(sA, sA.n), (sP, sP.n)], inc=sV)
                cw = [(sV, sV.n)]
                if oc == 0 and l1s["cb_free"]:
                    cw.append(l1s["cb_free"])
                b.op("tensor",
                     lambda e, oc=oc: e.matmul(
                         pbk[6][0:1, 256:512], onesc_bf[:, :], sq1[:, 0:256],
                         start=(oc == 0), stop=(oc == NCHUNK - 1)),
                     wait=cw, inc=sP)
            qfm_d = sA.at()
            if l1s["pend"]:
                emit_epi1(*l1s["pend"])
                l1s["pend"] = None
            _lq = int(_os.environ.get("L1QG", "9"))
            if _lq < 2:
                continue
            b.op("scalar",
                 lambda e: e.activation(out=rows1[32:33, 0:256], in_=pbk[6][0:1, 256:512],
                                        func=AF.Sqrt),
                 wait=[(sP, sP.n)], inc=sA)
            b.op("vector",
                 lambda e: e.reduce_max(rws1[0:1, 802:803], rows1[32:33, 0:256], axis=AX.X),
                 wait=[(sA, sA.n), kmax_d], inc=sV)
            b.op("vector",
                 lambda e: e.tensor_mul(rws1[0:1, 804:805], rws1[0:1, 802:803],
                                        rws1[0:1, 800:801]),
                 inc=sV)
            b.op("vector",
                 lambda e: e.tensor_scalar_mul(rws1[0:1, 806:807], rws1[0:1, 804:805],
                                               -1.0 / 32.0),
                 inc=sV)
            cb_mm_w = [(sV, sV.n)]
            if l1s["cb_free"]:
                cb_mm_w.append(l1s["cb_free"])
            b.op("tensor",
                 lambda e: e.matmul(pbk[7][0:128, 256:257], onesr_f[0:1, :],
                                    rws1[0:1, 806:807], start=True, stop=True),
                 wait=cb_mm_w, inc=sP)
            b.op("scalar",
                 lambda e: e.activation(out=cb[:, :], in_=pbk[7][0:128, 256:257],
                                        func=AF.Copy),
                 wait=[(sP, sP.n)], inc=sA)
            cb_d = sA.at()

            if _lq < 3:
                continue
            nk = 2 * (qg + 1)
            L = nk - 1
            expd = {}
            avdn1 = {}
            pav1 = (sP, sP.n)
            for i in range(L + 3):
                if i <= L:
                    kt = i
                    diag1 = (kt == 2 * qg)
                    diag2 = (kt == 2 * qg + 1)
                    qoff = 128 if diag2 else 0
                    w = 256 - qoff
                    scb = pbk[i % 2]
                    wt = [qfm_d]
                    if i >= 2:
                        wt.append(expd[i - 2])
                    for c in range(NCHUNK):
                        b.op("tensor",
                             lambda e, kt=kt, qoff=qoff, w=w, scb=scb, c=c: e.matmul(
                                 scb[0:128, 0:w],
                                 kfm[c][:, kt * 128:(kt + 1) * 128],
                                 qfm[:, c * 256 + qoff:c * 256 + 256],
                                 start=(c == 0), stop=(c == NCHUNK - 1)),
                             wait=wt if c == 0 else [], inc=sP)
                    scd = sP.at()
                    if diag1 or diag2:
                        b.op("vector",
                             lambda e, scb=scb: e.tensor_add(
                                 scb[0:128, 0:128], scb[0:128, 0:128], tri_sb[:]),
                             wait=[scd], inc=sV)
                    ew = [scd, cb_d] + ([(sV, sV.n)] if (diag1 or diag2) else [])
                    ew.append(avdn1[i - 3] if i >= 3 else pav1)
                    b.op("scalar",
                         lambda e, i=i, w=w, scb=scb: e.activation(
                             out=ET[:, (i % 3) * 256:(i % 3) * 256 + w],
                             in_=scb[0:128, 0:w], func=AF.Exp,
                             scale=1.0 / 32.0, bias=cb[:, 0:1]),
                         wait=ew, inc=sA)
                    expd[i] = sA.at()
                j = i - 1
                if _lq < 4:
                    j = -1
                if 0 <= j <= L:
                    kt = j
                    qoff = 128 if (kt == 2 * qg + 1) else 0
                    w = 256 - qoff
                    wt = [expd[j]]
                    if j == 0 and l1s["av_free"]:
                        wt.append(l1s["av_free"])
                    for oc in range(NCHUNK):
                        avb = pbk[2 + oc // 2]
                        aoff = (oc % 2) * 256
                        b.op("tensor",
                             lambda e, kt=kt, qoff=qoff, w=w, j=j, L=L, oc=oc, avb=avb, aoff=aoff: e.matmul(
                                 avb[0:128, aoff + qoff:aoff + 256],
                                 vtm1[:, kt * 1024 + oc * 128:kt * 1024 + (oc + 1) * 128],
                                 ET[:, (kt % 3) * 256:(kt % 3) * 256 + w],
                                 start=(j == 0 and oc % 2 == 0), stop=(j == L)),
                             wait=wt if oc == 0 else [], inc=sP)
                    b.op("tensor",
                         lambda e, kt=kt, qoff=qoff, w=w, j=j, L=L: e.matmul(
                             pbk[6][0:1, qoff:256], onesc_bf[:, :],
                             ET[:, (kt % 3) * 256:(kt % 3) * 256 + w],
                             start=(j == 0), stop=(j == L)),
                         inc=sP)
                    avdn1[j] = sP.at()
            avd = sP.at()
            if _lq < 5:
                continue
            b.op("vector",
                 lambda e: e.reciprocal(rws1[0:1, 0:256], pbk[6][0:1, 0:256]),
                 wait=[avd], inc=sV)
            b.op("scalar",
                 lambda e: e.activation(out=hlr1[0:1, 0:256], in_=rws1[0:1, 0:256],
                                        func=AF.Copy),
                 wait=[(sV, sV.n)], inc=sA)
            b.op("vector",
                 lambda e: e.tensor_sub(rws1[0:1, 512:768], rws1[0:1, 0:256],
                                        hlr1[0:1, 0:256]),
                 wait=[(sA, sA.n)], inc=sV)
            b.op("scalar",
                 lambda e: e.activation(out=hlr1[0:1, 256:512], in_=rws1[0:1, 512:768],
                                        func=AF.Copy),
                 wait=[(sV, sV.n)], inc=sA)
            hilo = sA.at()
            l1s["pend"] = (qg, hilo)
            l1s["qfm_free"] = sP.at()
        if l1s["pend"]:
            emit_epi1(*l1s["pend"])
            l1s["pend"] = None
        bar()
        if _os.environ.get("ATTDBG") == "1":
            hh_ = nc.dram_tensor("d_et", [128, 3 * 256], BF16,
                                 kind="ExternalOutput").ap()
            b.dma("gpsimd", hh_, ET[:], inc=sG)
            hh_ = nc.dram_tensor("d_cb", [128, 1], F32,
                                 kind="ExternalOutput").ap()
            b.dma("gpsimd", hh_, cb[:], inc=sG)
            hh_ = nc.dram_tensor("d_dbc1", [128, 256], F32,
                                 kind="ExternalOutput").ap()
            b.dma("gpsimd", hh_, dbc1[0][:], inc=sG)
            hh_ = nc.dram_tensor("d_vtm1", [128, 16 * 1024], BF16,
                                 kind="ExternalOutput").ap()
            b.dma("gpsimd", hh_, vtm1[:], inc=sG)
            hh_ = nc.dram_tensor("d_kfm0", [128, T], BF16,
                                 kind="ExternalOutput").ap()
            b.dma("gpsimd", hh_, kfm[1][:], inc=sG)
            bar()

    def emit_l1_ffn():
        pool = Pool()
        hid = [pool.t(f"f_h{i}", [128, T], BF16) for i in range(12)]
        stg_g = pool.t("f_sg_i", [128, 1024], I32)
        stg_u = pool.t("f_su_i", [128, 1024], I32)
        cgu = [pool.t(f"f_cgu{i}", [128, 2048], BF16) for i in range(2)]
        wg_b = [pool.t(f"f_wg{i}", [128, 8 * 128], BF16) for i in range(2)]
        wu_b = [pool.t(f"f_wu{i}", [128, 8 * 128], BF16) for i in range(2)]
        sg = [pool.t(f"f_sgl{i}", [128, 512], BF16) for i in range(2)]
        stg_d = pool.t("f_sd_i", [128, 1536], I32)
        wdc = [pool.t(f"f_wdc{i}", [128, 1536], BF16) for i in range(2)]
        wd_b = [pool.t(f"f_wd{i}", [128, 12 * 128], BF16) for i in range(2)]

        st = {"stg": None, "cgu_free": [None, None], "wb_free": [None, None],
              "tb_free": [None, None], "dstg": None, "wdc_free": [None, None],
              "wdb_free": [None, None], "cast": {}, "dcast": {}}
        bfree = [None] * 8
        sg_free = [None, None]
        hid_free = {}

        def gu_load(g, hl):
            h = g * 12 + hl
            wt = [st["stg"]] if st["stg"] else []
            b.dma("sync", stg_g[:], _chunks(W["wg_i"])[h], wait=wt, inc=sL)
            b.dma("sync", stg_u[:], _chunks(W["wu_i"])[h], inc=sL)
            ld = sL.at()
            buf = hl % 2
            cw = [ld] + ([st["cgu_free"][buf]] if st["cgu_free"][buf] else [])
            b.op("vector",
                 lambda e, buf=buf: e.tensor_copy(cgu[buf][:, 0:1024], stg_g[:]),
                 wait=cw, inc=sV)
            b.op("vector",
                 lambda e, buf=buf: e.tensor_copy(cgu[buf][:, 1024:2048], stg_u[:]),
                 inc=sV)
            st["stg"] = sV.at()
            st["cast"][hl] = sV.at()

        def gu_transpose(hl, ccs):
            buf = hl % 2
            for cc in ccs:
                bank = pbk[4 + cc % 2]
                tw = [st["cast"][hl]] if cc == ccs[0] else []
                if st["tb_free"][cc % 2]:
                    tw.append(st["tb_free"][cc % 2])
                if cc < 2 and st["wb_free"][buf]:
                    tw.append(st["wb_free"][buf])
                b.op("tensor",
                     lambda e, cc=cc, buf=buf, bank=bank: e.matmul(
                         bank[0:128, 0:128], cgu[buf][:, cc * 128:(cc + 1) * 128],
                         iden_bf[:], start=True, stop=True),
                     wait=tw, inc=sP)
                dstb = wg_b[buf] if cc < 8 else wu_b[buf]
                dcol = (cc % 8) * 128
                b.op("scalar",
                     lambda e, dstb=dstb, dcol=dcol, bank=bank: e.activation(
                         out=dstb[:, dcol:dcol + 128], in_=bank[0:128, 0:128],
                         func=AF.Copy),
                     wait=[(sP, sP.n)], inc=sA)
                st["tb_free"][cc % 2] = sA.at()
                if cc == 15:
                    st["cgu_free"][buf] = sP.at()

        def wd_load(g, oc):
            buf = oc % 2
            wt = [st["dstg"]] if st["dstg"] else []
            b.dma("gpsimd", stg_d[:],
                  W["wd_i"][oc * 128:(oc + 1) * 128, g * 1536:(g + 1) * 1536],
                  wait=wt, inc=sG)
            cw = [sG.at()] + ([st["wdc_free"][buf]] if st["wdc_free"][buf] else [])
            b.op("vector",
                 lambda e, buf=buf: e.tensor_copy(wdc[buf][:], stg_d[:]),
                 wait=cw, inc=sV)
            st["dstg"] = sV.at()
            st["dcast"][oc] = sV.at()

        def wd_transpose(oc, js):
            buf = oc % 2
            for j in js:
                bank = pbk[4 + j % 2]
                tw = [st["dcast"][oc]] if j == js[0] else []
                if st["tb_free"][j % 2]:
                    tw.append(st["tb_free"][j % 2])
                if j < 2 and st["wdb_free"][buf]:
                    tw.append(st["wdb_free"][buf])
                b.op("tensor",
                     lambda e, j=j, buf=buf, bank=bank: e.matmul(
                         bank[0:128, 0:128], wdc[buf][:, j * 128:(j + 1) * 128],
                         iden_bf[:], start=True, stop=True),
                     wait=tw, inc=sP)
                b.op("scalar",
                     lambda e, j=j, buf=buf, bank=bank: e.activation(
                         out=wd_b[buf][:, j * 128:(j + 1) * 128],
                         in_=bank[0:128, 0:128], func=AF.Copy),
                     wait=[(sP, sP.n)], inc=sA)
                st["tb_free"][j % 2] = sA.at()
                if j == 11:
                    st["wdc_free"][buf] = sP.at()

        for g in range(2):
            gu_load(g, 0)
            gu_transpose(0, list(range(16)))
            for hl in range(12):
                buf = hl % 2
                if hl < 11:
                    gu_load(g, hl + 1)
                trd = st["tb_free"][0]
                for t in range(NT):
                    gbank = pbk[t % 2]
                    ubank = pbk[2 + t % 2]
                    for c in range(NCHUNK):
                        wt = []
                        if c == 0:
                            wt = ([bfree[t % 2]] if bfree[t % 2] else []) + [(sA, sA.n)]
                        b.op("tensor",
                             lambda e, c=c, t=t, buf=buf, gbank=gbank: e.matmul(
                                 gbank[:, :], wg_b[buf][:, c * 128:(c + 1) * 128],
                                 lnx[c][:, t * 512:(t + 1) * 512],
                                 start=(c == 0), stop=(c == NCHUNK - 1)),
                             wait=wt, inc=sP)
                    gd = (sP, sP.n)
                    swt = [gd] + ([sg_free[t % 2]] if sg_free[t % 2] else [])
                    b.op("scalar",
                         lambda e, gbank=gbank, t=t: e.activation(
                             out=sg[t % 2][:], in_=gbank[:, :], func=AF.Silu,
                             scale=1.0 / 127.0),
                         wait=swt, inc=sA)
                    silud = sA.at()
                    bfree[t % 2] = silud
                    for c in range(NCHUNK):
                        wt = []
                        if c == 0:
                            wt = [bfree[2 + t % 2]] if bfree[2 + t % 2] else []
                        b.op("tensor",
                             lambda e, c=c, t=t, buf=buf, ubank=ubank: e.matmul(
                                 ubank[:, :], wu_b[buf][:, c * 128:(c + 1) * 128],
                                 lnx[c][:, t * 512:(t + 1) * 512],
                                 start=(c == 0), stop=(c == NCHUNK - 1)),
                             wait=wt, inc=sP)
                    ud = (sP, sP.n)
                    hw = [ud, silud]
                    if hid_free.get(hl):
                        hw.append(hid_free[hl])
                    b.op("vector",
                         lambda e, hl=hl, t=t, ubank=ubank: e.scalar_tensor_tensor(
                             out=hid[hl][:, t * 512:(t + 1) * 512],
                             in0=ubank[:, :], scalar=1.0 / 127.0, in1=sg[t % 2][:],
                             op0=ALU.mult, op1=ALU.mult),
                         wait=hw, inc=sV)
                    bfree[2 + t % 2] = (sV, sV.n)
                    sg_free[t % 2] = (sV, sV.n)
                    if hl < 11:
                        gu_transpose(hl + 1, list(range(4 * t, 4 * t + 4)))
                st["wb_free"][buf] = (sP, sP.n)
            hidd = sV.at()
            wd_load(g, 0)
            wd_transpose(0, list(range(12)))
            for oc in range(NCHUNK):
                buf = oc % 2
                if oc < 7:
                    wd_load(g, oc + 1)
                for t in range(NT):
                    bank = pbk[6 + t % 2]
                    for hl in range(12):
                        wt = []
                        if hl == 0:
                            wt = ([bfree[6 + t % 2]] if bfree[6 + t % 2] else []) + [(sA, sA.n)]
                            if oc == 0 and t == 0:
                                wt.append(hidd)
                        b.op("tensor",
                             lambda e, hl=hl, t=t, buf=buf, bank=bank: e.matmul(
                                 bank[:, :], wd_b[buf][:, hl * 128:(hl + 1) * 128],
                                 hid[hl][:, t * 512:(t + 1) * 512],
                                 start=(hl == 0), stop=(hl == 11)),
                             wait=wt, inc=sP)
                    dstop = (sP, sP.n)
                    b.op("vector",
                         lambda e, oc=oc, t=t, bank=bank: e.scalar_tensor_tensor(
                             out=stream[oc][:, t * 512:(t + 1) * 512],
                             in0=bank[:, :], scalar=1.0 / 127.0,
                             in1=stream[oc][:, t * 512:(t + 1) * 512],
                             op0=ALU.mult, op1=ALU.add),
                         wait=[dstop], inc=sV)
                    bfree[6 + t % 2] = (sV, sV.n)
                    if oc < 7 and t < 3:
                        wd_transpose(oc + 1, list(range(4 * t, 4 * t + 4)))
                st["wdb_free"][buf] = (sP, sP.n)
            for hl in range(12):
                hid_free[hl] = (sP, sP.n)
        bar()

    if 0 in blocks:
        emit_ln()
        emit_l0_attn()
    if 1 in blocks:
        emit_ln()
        emit_moe()
    if 2 in blocks:
        emit_ln()
        emit_l1_attn()
    if 3 in blocks:
        emit_ln()
        emit_l1_ffn()

    pool = Pool()
    otm = [pool.t(f"otm{i}", [128, D], F32) for i in range(2)]
    st_hist = []
    for tc in range(NT128):
        buf = otm[tc % 2]
        for fc in range(NCHUNK):
            bank = pbk[fc % 2]
            b.op("tensor",
                 lambda e, fc=fc, tc=tc, bank=bank: e.matmul(
                     bank[0:128, 0:128], stream[fc][:, tc * 128:(tc + 1) * 128],
                     iden_f[:], start=True, stop=True),
                 wait=[(sA, max(0, sA.n - 1)), (sV, sV.n)], inc=sP)
            evw = [(sP, sP.n)]
            if fc == 0 and tc >= 2:
                evw.append((sG, st_hist[tc - 2]))
            b.op("scalar",
                 lambda e, fc=fc, buf=buf, bank=bank: e.activation(
                     out=buf[:, fc * 128:(fc + 1) * 128],
                     in_=bank[0:128, 0:128], func=AF.Copy),
                 wait=evw, inc=sA)
        b.dma("gpsimd", out_h[tc * 128:(tc + 1) * 128, :], buf[:],
              wait=[(sA, sA.n)], inc=sG)
        st_hist.append(sG.n)
    b.op("gpsimd", lambda e: e.engine_nop(), wait=[(sG, sG.n)])

    with nc.Block() as block:
        b.finalize(block)
    nc._b = b
    return nc


def _consts():
    ident = np.eye(128)
    tri = np.zeros((128, 128), dtype=np.float32)
    for k in range(128):
        tri[k, 0:k] = NEG
    sel = np.zeros((8, 8 * 128), dtype=np.float32)
    for e in range(8):
        sel[e, e * 128:(e + 1) * 128] = 1.0
    return {
        "c_ident_bf": ident.astype(ml_dtypes.bfloat16),
        "c_ident_f": ident.astype(np.float32),
        "c_tri": tri.astype(ml_dtypes.bfloat16),
        "c_sel": sel.astype(ml_dtypes.bfloat16),
    }


_WMAP = {
    0: ["L0_attn_w_attn", "L0_attn_b_attn", "L0_attn_w_proj", "L0_attn_b_proj"],
    1: ["L0_ffn_gate_w", "L0_ffn_w_gate", "L0_ffn_w_up", "L0_ffn_w_down",
        "L0_ffn_ws_gate", "L0_ffn_ws_up", "L0_ffn_ws_down"],
    2: ["L1_attn_Wk", "L1_attn_Wq", "L1_attn_Wv"],
    3: ["L1_ffn_w_gate", "L1_ffn_w_up", "L1_ffn_w_down"],
}


def kernel(**inputs):
    consts = _consts()
    x = np.ascontiguousarray(inputs["L0_attn_x"], dtype=np.float32)
    launches = [[0, 1, 2, 3]] if FUSED else [[0], [1], [2], [3]]
    cur = [x[c] for c in range(8)]
    for blocks in launches:
        nc = build_program(blocks)
        names = [n for blk in blocks for n in _WMAP[blk]]
        wmap = {n: np.ascontiguousarray(inputs[n]) for n in names}
        in_maps = [dict(x=np.ascontiguousarray(cur[c]), **wmap, **consts)
                   for c in range(8)]
        res = run_bass_kernel_spmd(nc, in_maps, core_ids=list(range(8)))
        cur = [res.results[c]["out"] for c in range(8)]
    return np.stack(cur).astype(np.float32)


# version 09
# speedup vs baseline: 1.1959x; 1.1959x over previous
import numpy as np
import ml_dtypes
import concourse.bass as bass
import concourse.mybir as mybir
from concourse.bass_utils import run_bass_kernel_spmd

F32 = mybir.dt.float32
BF16 = mybir.dt.bfloat16
I32 = mybir.dt.int32
AF = mybir.ActivationFunctionType
ALU = mybir.AluOpType
AX = mybir.AxisListType

T = 2048
D = 1024
NCHUNK = D // 128
NT = T // 512
NT128 = T // 128
NEG = -1.0e30

FUSED = True
DBG = {}


class Sem:
    def __init__(self, h, name):
        self.h = h
        self.name = name
        self.n = 0

    def at(self):
        return (self, self.n)


class B:
    ENGINES = ("sync", "tensor", "scalar", "vector", "gpsimd")

    def __init__(self, nc):
        self.nc = nc
        self.ops = {e: [] for e in self.ENGINES}
        self.last = {e: {} for e in self.ENGINES}
        self.autow = {}
        self.trace = {}

    def sem(self, name):
        return Sem(self.nc.alloc_semaphore(name), name)

    def op(self, eng, fn, wait=(), inc=None, inc_n=1, tag="", fwait=()):
        lw = self.last[eng]
        if eng in self.autow:
            s_ = self.autow[eng]
            wait = list(wait) + [(s_, s_.n)]
        waits = []
        wmeta = []
        for s, v in fwait:
            waits.append((s.h, v))
            wmeta.append((s.name, v))
            if lw.get(s.name, -1) < v:
                lw[s.name] = v
        for s, v in wait:
            if v > 0 and lw.get(s.name, -1) < v:
                waits.append((s.h, v))
                wmeta.append((s.name, v))
                lw[s.name] = v
        if inc is not None:
            inc.n += inc_n
            ih, iv = inc.h, inc_n
            imeta = (inc.name, inc_n)
        else:
            ih, iv = None, 0
            imeta = None
        self.trace.setdefault(eng, []).append((wmeta, imeta, tag))

        def run(e, fn=fn, waits=tuple(waits), ih=ih, iv=iv):
            for h, v in waits:
                e.wait_ge(h, v)
            ins = fn(e)
            if ih is not None:
                ins.then_inc(ih, iv)
        self.ops[eng].append(run)
        return inc.n if inc is not None else None

    def dma(self, eng, out, in_, wait=(), inc=None):
        return self.op(eng, lambda e, o=out, i=in_: e.dma_start(out=o, in_=i),
                       wait=wait, inc=inc, inc_n=16)

    def barrier(self, sems):
        for eng in self.ENGINES:
            for s in sems:
                if s.n > 0 and self.last[eng].get(s.name, -1) < s.n:
                    self.last[eng][s.name] = s.n
                    self.trace.setdefault(eng, []).append(
                        ([(s.name, s.n)], None, "barrier"))
                    self.ops[eng].append(lambda e, h=s.h, v=s.n: e.wait_ge(h, v))

    def finalize(self, block):
        for eng in self.ENGINES:
            ops = self.ops[eng]
            if not ops:
                continue

            def body(e, ops=ops):
                for f in ops:
                    f(e)
            getattr(block, eng)(body)


def _chunks(ap2d):
    return ap2d.rearrange("(c p) n -> c p n", p=128)


def build_program(blocks):
    nc = bass.Bass("TRN2", target_bir_lowering=False)
    b = B(nc)

    x_h = nc.dram_tensor("x", [T, D], F32, kind="ExternalInput").ap()
    out_h = nc.dram_tensor("out", [T, D], F32, kind="ExternalOutput").ap()
    iden_bf_h = nc.dram_tensor("c_ident_bf", [128, 128], BF16, kind="ExternalInput").ap()
    iden_f_h = nc.dram_tensor("c_ident_f", [128, 128], F32, kind="ExternalInput").ap()
    tri_h = nc.dram_tensor("c_tri", [128, 128], BF16, kind="ExternalInput").ap()
    sel_h = nc.dram_tensor("c_sel", [8, 8 * 128], BF16, kind="ExternalInput").ap()

    W = {}
    if 0 in blocks:
        W["w_attn"] = nc.dram_tensor("L0_attn_w_attn", [D, 3 * D], F32, kind="ExternalInput").ap()
        W["b_attn"] = nc.dram_tensor("L0_attn_b_attn", [3 * D], F32, kind="ExternalInput").ap()
        W["w_proj"] = nc.dram_tensor("L0_attn_w_proj", [D, D], F32, kind="ExternalInput").ap()
        W["b_proj"] = nc.dram_tensor("L0_attn_b_proj", [D], F32, kind="ExternalInput").ap()
    if 1 in blocks:
        W["gate_w"] = nc.dram_tensor("L0_ffn_gate_w", [D, 8], F32, kind="ExternalInput").ap()
        W["w_gate"] = nc.dram_tensor("L0_ffn_w_gate", [8, D, 2048], F32, kind="ExternalInput").ap()
        W["w_up"] = nc.dram_tensor("L0_ffn_w_up", [8, D, 2048], F32, kind="ExternalInput").ap()
        W["w_down"] = nc.dram_tensor("L0_ffn_w_down", [8, 2048, D], F32, kind="ExternalInput").ap()
        W["ws_gate"] = nc.dram_tensor("L0_ffn_ws_gate", [D, 2048], F32, kind="ExternalInput").ap()
        W["ws_up"] = nc.dram_tensor("L0_ffn_ws_up", [D, 2048], F32, kind="ExternalInput").ap()
        W["ws_down"] = nc.dram_tensor("L0_ffn_ws_down", [2048, D], F32, kind="ExternalInput").ap()
    if 2 in blocks:
        W["wk"] = nc.dram_tensor("L1_attn_Wk", [D, D], F32, kind="ExternalInput").ap()
        W["wq"] = nc.dram_tensor("L1_attn_Wq", [D, D], F32, kind="ExternalInput").ap()
        W["wv"] = nc.dram_tensor("L1_attn_Wv", [D, D], F32, kind="ExternalInput").ap()
    if 3 in blocks:
        W["wg_i"] = nc.dram_tensor("L1_ffn_w_gate", [3072, D], I32, kind="ExternalInput").ap()
        W["wu_i"] = nc.dram_tensor("L1_ffn_w_up", [3072, D], I32, kind="ExternalInput").ap()
        W["wd_i"] = nc.dram_tensor("L1_ffn_w_down", [D, 3072], I32, kind="ExternalInput").ap()

    stream = [nc.alloc_sbuf_tensor(f"stream{c}", [128, T], F32) for c in range(NCHUNK)]
    lnx = [nc.alloc_sbuf_tensor(f"lnx{c}", [128, T], BF16) for c in range(NCHUNK)]
    statr = nc.alloc_sbuf_tensor("statr", [128, T], F32)
    iden_bf = nc.alloc_sbuf_tensor("iden_bf", [128, 128], BF16)
    iden_f = nc.alloc_sbuf_tensor("iden_f", [128, 128], F32)
    tri_sb = nc.alloc_sbuf_tensor("tri_sb", [128, 128], BF16)
    sel_sb = nc.alloc_sbuf_tensor("sel_sb", [8, 8 * 128], BF16)
    ones_col_f = nc.alloc_sbuf_tensor("ones_col_f", [128, 1], F32)
    ones_row_bf = nc.alloc_sbuf_tensor("ones_row_bf", [33, 128], BF16)
    onesr_bf = nc.alloc_sbuf_tensor("onesr_bf", [1, 512], BF16)
    onesr_f = nc.alloc_sbuf_tensor("onesr_f", [1, 128], F32)
    ones64_bf = nc.alloc_sbuf_tensor("ones64_bf", [64, 1], BF16)
    onesc_bf = nc.alloc_sbuf_tensor("onesc_bf", [128, 1], BF16)
    ones_11 = nc.alloc_sbuf_tensor("ones_11", [1, 1], F32)
    ones65_f = nc.alloc_sbuf_tensor("ones65_f", [65, 1], F32)
    bproj_sb = nc.alloc_sbuf_tensor("bproj_sb", [128, 8], F32)
    eps_sb = nc.alloc_sbuf_tensor("eps_sb", [1, 1], F32)

    pool_lo = (nc.sbuf_base + 511) & ~511
    pool_hi = nc.sbuf_top - 64

    class Pool:
        def __init__(self, off=None):
            self.off = pool_lo if off is None else off

        def t(self, name, shape, dtype):
            per_part = int(np.prod(shape[1:])) * mybir.dt.size(dtype)
            off = (self.off + 255) & ~255
            assert off + per_part <= pool_hi, (
                f"pool overflow at {name}: {off + per_part} > {pool_hi}")
            h = nc.alloc_sbuf_tensor_at(f"{name}_{nc.next_id()}", list(shape),
                                        dtype, offset=off)
            self.off = off + per_part
            return h

        def fork(self):
            return Pool(self.off)

    pbk = [nc.alloc_psum_tensor(f"pb{i}", [128, 512], F32) for i in range(8)]

    sL = b.sem("sL")
    sG = b.sem("sG")
    sP = b.sem("sP")
    sA = b.sem("sA")
    sV = b.sem("sV")
    ALLSEMS = (sL, sG, sP, sA, sV)
    b.autow["vector"] = sV

    def bar():
        b.barrier(ALLSEMS)

    b.dma("sync", iden_bf[:], iden_bf_h, inc=sL)
    b.dma("sync", iden_f[:], iden_f_h, inc=sL)
    b.dma("sync", tri_sb[:], tri_h, inc=sL)
    b.dma("sync", sel_sb[:], sel_h, inc=sL)
    if 0 in blocks:
        for c in range(8):
            b.dma("sync", bproj_sb[:, c:c + 1],
                  W["b_proj"][c * 128:(c + 1) * 128][:, None], inc=sL)
    b.op("vector", lambda e: e.memset(ones_col_f[:], 1.0), inc=sV)
    b.op("vector", lambda e: e.memset(ones_row_bf[:], 1.0), inc=sV)
    b.op("vector", lambda e: e.memset(onesr_bf[:], 1.0), inc=sV)
    b.op("vector", lambda e: e.memset(onesr_f[:], 1.0), inc=sV)
    b.op("vector", lambda e: e.memset(ones64_bf[:], 1.0), inc=sV)
    b.op("vector", lambda e: e.memset(onesc_bf[:], 1.0), inc=sV)
    b.op("vector", lambda e: e.memset(ones_11[:], 1.0), inc=sV)
    b.op("vector", lambda e: e.memset(ones65_f[:], 1.0), inc=sV)
    b.op("vector", lambda e: e.memset(eps_sb[:], 1e-5), inc=sV)
    bar()

    pool = Pool()
    xin = [pool.t(f"xin{i}", [128, D], F32) for i in range(2)]
    tp_done = []
    for tc in range(NT128):
        buf = xin[tc % 2]
        wt = [(sP, tp_done[tc - 2])] if tc >= 2 else []
        ld = b.dma("sync", buf[:], x_h[tc * 128:(tc + 1) * 128, :], wait=wt, inc=sL)
        for fc in range(NCHUNK):
            bank = pbk[fc % 2]
            b.op("tensor",
                 lambda e, buf=buf, fc=fc, bank=bank: e.matmul(
                     bank[0:128, 0:128], buf[:, fc * 128:(fc + 1) * 128],
                     iden_f[:], start=True, stop=True),
                 wait=[(sL, ld), (sA, max(0, sA.n - 1))], inc=sP)
            b.op("scalar",
                 lambda e, fc=fc, tc=tc, bank=bank: e.activation(
                     out=stream[fc][:, tc * 128:(tc + 1) * 128],
                     in_=bank[0:128, 0:128], func=AF.Copy),
                 wait=[(sP, sP.n)], inc=sA)
        tp_done.append(sP.n)
    bar()

    def emit_ln():
        lpool = Pool()
        sq = lpool.t("ln_sq", [128, 2 * 512], F32)
        rowbf = lpool.t("ln_rowbf", [33, T], BF16)
        msq = lpool.t("ln_msq", [1, 2 * T], F32)
        mb = lpool.t("ln_mb", [128, T], BF16)
        rb = lpool.t("ln_rb", [128, T], BF16)
        base = (sP, sP.n)
        sq_free = [None, None]
        mmd = {}

        def stats(t):
            for c in range(NCHUNK):
                slot = sq[:, (c % 2) * 512:(c % 2) * 512 + 512]
                sw = [base]
                if sq_free[c % 2]:
                    sw.append(sq_free[c % 2])
                b.op("scalar",
                     lambda e, c=c, t=t, slot=slot: e.activation(
                         out=slot, in_=stream[c][:, t * 512:(t + 1) * 512],
                         func=AF.Square),
                     wait=sw, inc=sA)
                sqd = sA.at()
                b.op("tensor",
                     lambda e, c=c, t=t: e.matmul(
                         pbk[t][0:1, :], ones_col_f[:],
                         stream[c][:, t * 512:(t + 1) * 512],
                         start=(c == 0), stop=(c == NCHUNK - 1)),
                     wait=[base] if c == 0 else [], inc=sP)
                b.op("tensor",
                     lambda e, c=c, t=t, slot=slot: e.matmul(
                         pbk[4 + t][0:1, :], ones_col_f[:], slot,
                         start=(c == 0), stop=(c == NCHUNK - 1)),
                     wait=[sqd], inc=sP)
                sq_free[c % 2] = (sP, sP.n)
            mmp = sP.at()
            b.op("scalar",
                 lambda e, t=t: e.activation(
                     out=statr[0:1, t * 512:(t + 1) * 512], in_=pbk[t][0:1, :],
                     func=AF.Identity, scale=1.0 / D),
                 wait=[mmp], inc=sA)
            b.op("scalar",
                 lambda e, t=t: e.activation(
                     out=msq[0:1, t * 512:(t + 1) * 512], in_=pbk[4 + t][0:1, :],
                     func=AF.Identity, scale=1.0 / D),
                 inc=sA)
            mmd[t] = sA.at()

        def chainplus(t):
            b.op("vector",
                 lambda e, t=t: e.tensor_mul(
                     msq[0:1, T + t * 512:T + (t + 1) * 512],
                     statr[0:1, t * 512:(t + 1) * 512],
                     statr[0:1, t * 512:(t + 1) * 512]),
                 wait=[mmd[t]], inc=sV)
            b.op("vector",
                 lambda e, t=t: e.tensor_sub(
                     msq[0:1, t * 512:(t + 1) * 512],
                     msq[0:1, t * 512:(t + 1) * 512],
                     msq[0:1, T + t * 512:T + (t + 1) * 512]),
                 inc=sV)
            vd = sV.at()
            b.op("scalar",
                 lambda e, t=t: e.activation(
                     out=statr[64:65, t * 512:(t + 1) * 512],
                     in_=msq[0:1, t * 512:(t + 1) * 512],
                     func=AF.Sqrt, bias=eps_sb[0:1, 0:1]),
                 wait=[vd], inc=sA)
            b.op("vector",
                 lambda e, t=t: e.reciprocal(
                     statr[64:65, t * 512:(t + 1) * 512],
                     statr[64:65, t * 512:(t + 1) * 512]),
                 wait=[sA.at()], inc=sV)
            rec = sV.at()
            b.op("scalar",
                 lambda e, t=t: e.activation(
                     out=rowbf[0:1, t * 512:(t + 1) * 512],
                     in_=statr[0:1, t * 512:(t + 1) * 512], func=AF.Copy),
                 wait=[rec], inc=sA)
            b.op("scalar",
                 lambda e, t=t: e.activation(
                     out=rowbf[32:33, t * 512:(t + 1) * 512],
                     in_=statr[64:65, t * 512:(t + 1) * 512], func=AF.Copy),
                 inc=sA)
            rbf = sA.at()
            for dst, row, bank in ((mb, 0, pbk[t]), (rb, 1, pbk[4 + t])):
                b.op("tensor",
                     lambda e, row=row, t=t, bank=bank: e.matmul(
                         bank[0:128, :], ones_row_bf[32 * row:32 * row + 1, :],
                         rowbf[32 * row:32 * row + 1, t * 512:(t + 1) * 512],
                         start=True, stop=True),
                     wait=[rbf], inc=sP)
                b.op("scalar",
                     lambda e, dst=dst, t=t, bank=bank: e.activation(
                         out=dst[:, t * 512:(t + 1) * 512], in_=bank[0:128, :],
                         func=AF.Copy),
                     wait=[(sP, sP.n)], inc=sA)
            bc = sA.at()
            for c in range(NCHUNK):
                b.op("vector",
                     lambda e, c=c, t=t: e.tensor_sub(
                         lnx[c][:, t * 512:(t + 1) * 512],
                         stream[c][:, t * 512:(t + 1) * 512],
                         mb[:, t * 512:(t + 1) * 512]),
                     wait=[bc] if c == 0 else [], inc=sV)
                b.op("vector",
                     lambda e, c=c, t=t: e.tensor_mul(
                         lnx[c][:, t * 512:(t + 1) * 512],
                         lnx[c][:, t * 512:(t + 1) * 512],
                         rb[:, t * 512:(t + 1) * 512]),
                     inc=sV)

        for t in range(NT):
            stats(t)
            if t >= 1:
                chainplus(t - 1)
        chainplus(NT - 1)
        bar()

    def emit_l0_attn():
        pool = Pool()
        vtm = pool.t("a_vtm", [128, 16 * 520], BF16)
        qa = pool.t("a_qa", [128, T], BF16)
        qb = pool.t("a_qb", [128, T], BF16)
        ka = pool.t("a_ka", [128, T], BF16)
        kb = pool.t("a_kb", [128, T], BF16)
        rows = pool.t("a_rows", [128, T], BF16)
        rows2 = pool.t("a_rows2", [64, T], BF16)
        rws = pool.t("a_rws", [1, 1048], F32)
        ebuf = pool.t("a_ebuf", [128, 4 * 512], BF16)
        dbc = [pool.t(f"a_dbc{i}", [64, 512], F32) for i in range(2)]
        ybf = [pool.t(f"a_ybf{i}", [128, T], BF16) for i in range(4)]
        sqb = pool.t("a_sqb", [64, 2 * 512], BF16)
        hlr = pool.t("a_hlr", [1, 1024], BF16)
        bq2 = pool.t("a_bq2", [128, 2], F32)
        bvr = pool.t("a_bvr", [1, 1024], BF16)
        bvf = pool.t("a_bvf", [1, 1024], F32)
        wstg = pool.t("a_wstg", [128, 512], F32)
        wbf = pool.t("a_wbf", [128, 2048], BF16)
        pX = pool.fork()
        wvstg = pX.t("a_wvstg", [128, 1024], F32)
        wvbf = pX.t("a_wvbf", [128, 8 * 512], BF16)
        pX2 = pool.fork()
        pstg = pX2.t("a_pstg", [128, 512], F32)
        pbf = [pX2.t(f"a_pbf{i}", [128, 512], BF16) for i in range(2)]

        wat = _chunks(W["w_attn"])
        wpt = _chunks(W["w_proj"])

        st = {
            "wv_stg_free": None,
            "w_stg_free": None,
            "wbf_free": None,
            "vtm_free": None,
            "ybf_free": None,
            "pstg_free": None, "pbf_free": [None, None], "pj_bfree": [None, None],
            "bq_free": None,
            "x_free": None,
            "qk_banks": None,
            "b4_free": None,
            "bc_free": [None, None],
            "dbc_free": [None, None],
            "av_free": [None, None],
            "qa_free": None,
            "aug_ld": None,
            "qk_done": None,
        }
        b.op("vector", lambda e: e.memset(ka[64:65, :], 1.0), inc=sV)
        b.op("vector", lambda e: e.memset(kb[64:65, :], 1.0), inc=sV)

        def emit_vb(r):
            b.dma("sync", bvf[0:1, 0:512],
                  W["b_attn"][2 * D + r * 512:2 * D + (r + 1) * 512][None, :],
                  wait=[(sP, sP.n), (sA, sA.n)], inc=sL)
            b.op("scalar", lambda e: e.activation(out=bvr[0:1, 0:512], in_=bvf[0:1, 0:512],
                                                  func=AF.Copy),
                 wait=[sL.at()], inc=sA)
            b.op("vector", lambda e: e.tensor_sub(bvf[0:1, 512:1024], bvf[0:1, 0:512], bvr[0:1, 0:512]),
                 wait=[sA.at()], inc=sV)
            b.op("scalar", lambda e: e.activation(out=bvr[0:1, 512:1024], in_=bvf[0:1, 512:1024],
                                                  func=AF.Copy),
                 wait=[sV.at()], inc=sA)
            bias_d = sA.at()
            vw = [st["vtm_free"]] if st["vtm_free"] else []
            for s in range(16):
                b.op("vector",
                     lambda e, s=s: e.memset(
                         vtm[:, s * 520:(s + 1) * 520].rearrange(
                             "p (h x) -> p h x", x=65)[:, :, 64:65], 1.0),
                     wait=vw if s == 0 else [], inc=sV)
            ones_d = sV.at()
            for qtr in range(4):
                wt = [st["wv_stg_free"]] if st["wv_stg_free"] else []
                if qtr == 0 and st["x_free"]:
                    wt = wt + [st["x_free"]]
                b.dma("sync", wvstg[:].rearrange("p (c n) -> p c n", n=128),
                      wat[:, :, 2 * D + r * 512 + qtr * 128:
                          2 * D + r * 512 + (qtr + 1) * 128].rearrange("c p n -> p c n"),
                      wait=wt, inc=sL)
                cwv = [sL.at()]
                if qtr == 0:
                    cwv.append((sP, sP.n))
                b.op("scalar",
                     lambda e, qtr=qtr: e.activation(
                         out=wvbf[:].rearrange("p (c n) -> p c n", n=512)[
                             :, :, qtr * 128:(qtr + 1) * 128],
                         in_=wvstg[:].rearrange("p (c n) -> p c n", n=128),
                         func=AF.Copy),
                     wait=cwv, inc=sA)
                st["wv_stg_free"] = sA.at()
            cast_d = sA.at()
            vb_bank_free = [None, None]
            for t16 in range(NT128):
                bank = pbk[6 + t16 % 2]
                wt0 = [cast_d, (sV, sV.n)]
                if vb_bank_free[t16 % 2]:
                    wt0.append(vb_bank_free[t16 % 2])
                for c in range(NCHUNK):
                    b.op("tensor",
                         lambda e, c=c, t16=t16, bank=bank: e.matmul(
                             bank[:, :], lnx[c][:, t16 * 128:(t16 + 1) * 128],
                             wvbf[:, c * 512:(c + 1) * 512],
                             start=(c == 0), stop=False),
                         wait=wt0 if c == 0 else [], inc=sP)
                b.op("tensor",
                     lambda e, bank=bank: e.matmul(
                         bank[:, :], ones_row_bf[0:1, 0:128], bvr[0:1, 0:512],
                         start=False, stop=False),
                     wait=[bias_d] if t16 == 0 else [], inc=sP)
                b.op("tensor",
                     lambda e, bank=bank: e.matmul(
                         bank[:, :], ones_row_bf[0:1, 0:128], bvr[0:1, 512:1024],
                         start=False, stop=True),
                     inc=sP)
                mm_d = sP.at()
                b.op("vector",
                     lambda e, t16=t16, bank=bank: e.tensor_copy(
                         vtm[:, t16 * 520:(t16 + 1) * 520].rearrange(
                             "p (h x) -> p h x", x=65)[:, :, 0:64],
                         bank[:, :].rearrange("p (h d) -> p h d", d=64)),
                     wait=[mm_d, ones_d], inc=sV)
                vb_bank_free[t16 % 2] = sV.at()
            st["vtm_ready"] = (sV.at(), sV.at())

        def emit_qk_w(p):
            for wi, coff in ((0, p * 128), (1, D + p * 128)):
                for hf in range(2):
                    wt = [st["w_stg_free"]] if st["w_stg_free"] else []
                    b.dma("sync", wstg[:].rearrange("p (c n) -> p c n", n=128),
                          wat[4 * hf:4 * hf + 4, :, coff:coff + 128].rearrange("c p n -> p c n"),
                          wait=wt, inc=sL)
                    cw = [sL.at()]
                    if wi == 0 and hf == 0 and st["wbf_free"]:
                        cw.append(st["wbf_free"])
                    b.op("scalar",
                         lambda e, wi=wi, hf=hf: e.activation(
                             out=wbf[:, wi * 1024 + hf * 512:wi * 1024 + (hf + 1) * 512],
                             in_=wstg[:], func=AF.Copy),
                         wait=cw, inc=sA)
                    st["w_stg_free"] = sA.at()
            st["wcast"] = sA.at()
            bw0 = [st["bq_free"]] if st["bq_free"] else []
            b.dma("sync", bq2[:, 0:1], W["b_attn"][p * 128:(p + 1) * 128][:, None],
                  wait=bw0, inc=sL)
            b.dma("sync", bq2[:, 1:2],
                  W["b_attn"][D + p * 128:D + (p + 1) * 128][:, None], inc=sL)
            st["bias_d"] = sL.at()

        def emit_qk_main(p):
            wcast = st["wcast"]
            bias_d = st["bias_d"]
            qkb_free = [None, None]
            evd = {}
            for wi, (da, db_) in ((0, (qa, qb)), (1, (ka, kb))):
                for t in range(NT):
                    bank = pbk[(wi * NT + t) % 2]
                    wt0 = [wcast, bias_d]
                    if qkb_free[(wi * NT + t) % 2]:
                        wt0.append(qkb_free[(wi * NT + t) % 2])
                    if st["qk_banks"]:
                        wt0.append(st["qk_banks"])
                    for c in range(NCHUNK):
                        b.op("tensor",
                             lambda e, wi=wi, c=c, t=t, bank=bank: e.matmul(
                                 bank[:, :], wbf[:, wi * 1024 + c * 128:wi * 1024 + (c + 1) * 128],
                                 lnx[c][:, t * 512:(t + 1) * 512],
                                 start=(c == 0), stop=(c == NCHUNK - 1)),
                             wait=wt0 if c == 0 else [], inc=sP)
                    mm_d = sP.at()
                    ew = [mm_d, bias_d]
                    if st["qa_free"]:
                        ew.append(st["qa_free"])
                    b.op("scalar",
                         lambda e, da=da, t=t, wi=wi, bank=bank: e.activation(
                             out=da[0:64, t * 512:(t + 1) * 512], in_=bank[0:64, :],
                             func=AF.Identity, bias=bq2[0:64, wi:wi + 1]),
                         wait=ew, inc=sA)
                    b.op("scalar",
                         lambda e, db_=db_, t=t, wi=wi, bank=bank: e.activation(
                             out=db_[0:64, t * 512:(t + 1) * 512], in_=bank[64:128, :],
                             func=AF.Identity, bias=bq2[64:128, wi:wi + 1]),
                         inc=sA)
                    qkb_free[(wi * NT + t) % 2] = sA.at()
                    evd[(wi, t)] = sA.at()
            qk_mm_done = sP.at()
            st["bq_free"] = sA.at()
            st["wbf_free"] = qk_mm_done
            mm_done = {}
            sqrt_done = {}
            items = [(wi, hh, t) for wi in (0, 1) for hh in (0, 1) for t in range(NT)]
            for i, (wi, hh, t) in enumerate(items):
                src_ = (qa, qb, ka, kb)[wi * 2 + hh]
                slot = sqb[0:64, (i % 2) * 512:(i % 2) * 512 + 512]
                sw = [evd[(wi, t)]]
                if i >= 2:
                    sw.append(mm_done[i - 2])
                b.op("vector",
                     lambda e, src_=src_, t=t, slot=slot: e.tensor_mul(
                         slot, src_[0:64, t * 512:(t + 1) * 512],
                         src_[0:64, t * 512:(t + 1) * 512]),
                     wait=sw, inc=sV)
                bank = pbk[4 + i % 2]
                bw = [(sV, sV.n)]
                if i < 2:
                    if st["bc_free"][i % 2]:
                        bw.append(st["bc_free"][i % 2])
                if i >= 2:
                    bw.append(sqrt_done[i - 2])
                b.op("tensor",
                     lambda e, slot=slot, bank=bank: e.matmul(
                         bank[0:1, 0:512], ones64_bf[:, :], slot,
                         start=True, stop=True),
                     wait=bw, inc=sP)
                mm_done[i] = sP.at()
                rmap = {(0, 0): 0, (0, 1): 32, (1, 0): 64, (1, 1): 96}[(wi, hh)]
                b.op("scalar",
                     lambda e, rmap=rmap, t=t, bank=bank: e.activation(
                         out=rows[rmap:rmap + 1, t * 512:(t + 1) * 512],
                         in_=bank[0:1, 0:512], func=AF.Sqrt),
                     wait=[(sP, sP.n)], inc=sA)
                sqrt_done[i] = sA.at()
            st["b4_free"] = sA.at()
            rows_d = sA.at()
            for hh, qdst in ((0, qa), (1, qb)):
                b.op("vector",
                     lambda e, hh=hh: e.reduce_max(
                         rws[0:1, 4 * hh:4 * hh + 1],
                         rows[64 + 32 * hh:65 + 32 * hh, 0:T], axis=AX.X),
                     wait=[rows_d], inc=sV)
                b.op("vector",
                     lambda e, hh=hh: e.tensor_scalar_mul(
                         rws[0:1, 4 * hh + 1:4 * hh + 2], rws[0:1, 4 * hh:4 * hh + 1], -1.0),
                     inc=sV)
                b.op("vector",
                     lambda e, hh=hh: e.tensor_single_scalar(
                         rows2[32 * hh:32 * hh + 1, 0:T], rows[32 * hh:32 * hh + 1, 0:T],
                         rws[0:1, 4 * hh + 1:4 * hh + 2], op=ALU.mult),
                     inc=sV)
                aw = [(sV, sV.n)]
                if st["qa_free"]:
                    aw.append(st["qa_free"])
                if st.get("aug_prev"):
                    aw.append(st["aug_prev"])
                b.dma("sync", qdst[64:65, 0:T], rows2[32 * hh:32 * hh + 1, 0:T],
                      wait=aw, inc=sL)
            st["aug_prev"] = (sL, sL.n)
            st["aug_ld"] = sL.at()

        def emit_att(p):
            tiles = []
            for hh in (0, 1):
                for qg in range(4):
                    tl = [(t, 0, 512) for t in range(4 * qg)]
                    tl += [(4 * qg + j, j * 128, 512 - j * 128) for j in range(4)]
                    gidx = hh * 4 + qg
                    for ti, (kt, qoff, w) in enumerate(tl):
                        tiles.append((hh, qg, gidx, kt, qoff, w,
                                      ti == 0, ti == len(tl) - 1))
            expd = {}
            avdn = {}
            pend = st.setdefault("pend", [])
            pav = (sP, sP.n)
            tick0 = st.setdefault("ticker", 0)

            def emit_epiB(gidx, hh, qg, p_, hilo):
                bcb = pbk[4 + gidx % 2]
                av = pbk[2 + gidx % 2]
                yoff = 64 * hh
                bw = [hilo]
                if st["bc_free"][gidx % 2]:
                    bw.append(st["bc_free"][gidx % 2])
                if st["b4_free"]:
                    bw.append(st["b4_free"])
                    st["b4_free"] = None
                b.op("tensor",
                     lambda e, bcb=bcb: e.matmul(
                         bcb[0:64, 0:512], ones_row_bf[0:1, 0:64], hlr[0:1, 0:512],
                         start=True, stop=False),
                     wait=bw, inc=sP)
                b.op("tensor",
                     lambda e, bcb=bcb: e.matmul(
                         bcb[0:64, 0:512], ones_row_bf[0:1, 0:64], hlr[0:1, 512:1024],
                         start=False, stop=True),
                     inc=sP)
                dw = [(sP, sP.n)]
                if st["dbc_free"][gidx % 2]:
                    dw.append(st["dbc_free"][gidx % 2])
                b.op("scalar",
                     lambda e, gidx=gidx, bcb=bcb: e.activation(
                         out=dbc[gidx % 2][:, :], in_=bcb[0:64, 0:512], func=AF.Copy),
                     wait=dw, inc=sA)
                st["bc_free"][gidx % 2] = sA.at()
                yw = [(sA, sA.n)]
                if st["ybf_free"]:
                    yw.append(st["ybf_free"])
                av = pbk[2 + gidx % 2]
                b.op("vector",
                     lambda e, p=p_, yoff=yoff, qg=qg, av=av, gidx=gidx: e.tensor_mul(
                         ybf[p % 4][yoff:yoff + 64, qg * 512:(qg + 1) * 512],
                         av[0:64, 0:512], dbc[gidx % 2][:, :]),
                     wait=yw, inc=sV)
                st["av_free"][gidx % 2] = sV.at()
                st["dbc_free"][gidx % 2] = sV.at()

            n = len(tiles)
            for i in range(n + 2):
                st["ticker"] = tick0 + i
                while pend and pend[0][0] <= tick0 + i:
                    ent = pend.pop(0)
                    emit_epiB(*ent[1:])
                if i < n:
                    hh, qg, gidx, kt, qoff, w, first, last = tiles[i]
                    q_t = qa if hh == 0 else qb
                    k_t = ka if hh == 0 else kb
                    hloc = (p % 4) * 2 + hh
                    scb = pbk[i % 2]
                    wt = []
                    if i >= 2:
                        wt.append(expd[i - 2])
                    if i < 2:
                        wt += [(sA, sA.n), st["aug_ld"],
                               st["vtm_ready"][0], st["vtm_ready"][1]]
                        if st["qk_banks"]:
                            wt.append(st["qk_banks"])
                    b.op("tensor",
                         lambda e, kt=kt, qoff=qoff, w=w, scb=scb, q_t=q_t, k_t=k_t, qg=qg: e.matmul(
                             scb[0:128, 0:w],
                             k_t[0:65, kt * 128:(kt + 1) * 128],
                             q_t[0:65, qg * 512 + qoff:(qg + 1) * 512],
                             start=True, stop=True),
                         wait=wt, inc=sP)
                    scd = sP.at()
                    diag = kt >= 4 * qg
                    if diag:
                        b.op("vector",
                             lambda e, scb=scb: e.tensor_add(
                                 scb[0:128, 0:128], scb[0:128, 0:128], tri_sb[:]),
                             wait=[scd], inc=sV)
                    ew = [scd] + ([(sV, sV.n)] if diag else [])
                    if i >= 4:
                        ew.append(avdn[i - 4])
                    else:
                        ew.append(pav)
                    b.op("scalar",
                         lambda e, i=i, w=w, scb=scb: e.activation(
                             out=ebuf[:, (i % 4) * 512:(i % 4) * 512 + w],
                             in_=scb[0:128, 0:w], func=AF.Exp, scale=0.125),
                         wait=ew, inc=sA)
                    expd[i] = sA.at()
                j = i - 2
                if 0 <= j < n:
                    hh, qg, gidx, kt, qoff, w, first, last = tiles[j]
                    hloc = (p % 4) * 2 + hh
                    av = pbk[2 + gidx % 2]
                    wt = [expd[j]]
                    if first and st["av_free"][gidx % 2]:
                        wt.append(st["av_free"][gidx % 2])
                    b.op("tensor",
                         lambda e, kt=kt, qoff=qoff, w=w, j=j, av=av, hloc=hloc,
                                first=first, last=last: e.matmul(
                             av[0:65, qoff:512],
                             vtm[:, kt * 520 + hloc * 65:kt * 520 + hloc * 65 + 65],
                             ebuf[:, (j % 4) * 512:(j % 4) * 512 + w],
                             start=first, stop=last),
                         wait=wt, inc=sP)
                    avdn[j] = sP.at()
                    if last:
                        avd = sP.at()
                        b.op("vector",
                             lambda e, av=av: e.reciprocal(rws[0:1, 8:520],
                                                           av[64:65, 0:512]),
                             wait=[avd], inc=sV)
                        b.op("scalar",
                             lambda e: e.activation(out=hlr[0:1, 0:512],
                                                    in_=rws[0:1, 8:520], func=AF.Copy),
                             wait=[(sV, sV.n)], inc=sA)
                        b.op("vector",
                             lambda e: e.tensor_sub(rws[0:1, 528:1040], rws[0:1, 8:520],
                                                    hlr[0:1, 0:512]),
                             wait=[(sA, sA.n)], inc=sV)
                        b.op("scalar",
                             lambda e: e.activation(out=hlr[0:1, 512:1024],
                                                    in_=rws[0:1, 528:1040], func=AF.Copy),
                             wait=[(sV, sV.n)], inc=sA)
                        pend.append((tick0 + i + 2, gidx, hh, qg, p, sA.at()))
            st["ticker"] = tick0 + n + 2
            st["epiB"] = emit_epiB
            st["qa_free"] = sP.at()
            st["qk_banks"] = sA.at()
            st["vtm_free"] = sP.at()

        def emit_proj(r):
            import os as _os
            _pc = int(_os.environ.get("PROJCUT", "3"))
            for ent in st.get("pend", []):
                st["epiB"](*ent[1:])
            st["pend"] = []
            ymuls = sV.at()
            for oc in range(NCHUNK):
                wt = [st["pstg_free"]] if st["pstg_free"] else []
                if oc == 0 and st["wv_stg_free"]:
                    wt = wt + [st["wv_stg_free"]]
                b.dma("sync", pstg[:].rearrange("p (c n) -> p c n", n=128),
                      wpt[4 * r:4 * r + 4, :, oc * 128:(oc + 1) * 128].rearrange("c p n -> p c n"),
                      wait=wt, inc=sL)
                dst = oc % 2
                cw = [sL.at()]
                if oc == 0:
                    cw.append((sP, sP.n))
                if st["pbf_free"][dst]:
                    cw.append(st["pbf_free"][dst])
                b.op("scalar",
                     lambda e, dst=dst: e.activation(out=pbf[dst][:], in_=pstg[:],
                                                     func=AF.Copy),
                     wait=cw, inc=sA)
                st["pstg_free"] = sA.at()
                castd = sA.at()
                for t in range(NT):
                    if _pc < 2:
                        break
                    bank = pbk[t % 2]
                    wt0 = [castd]
                    if st["pj_bfree"][t % 2]:
                        wt0.append(st["pj_bfree"][t % 2])
                    if oc == 0 and t == 0:
                        wt0 += [ymuls, (sA, sA.n)]
                        if st["qk_banks"]:
                            wt0.append(st["qk_banks"])
                    for cc in range(4):
                        b.op("tensor",
                             lambda e, cc=cc, t=t, bank=bank, dst=dst: e.matmul(
                                 bank[:, :], pbf[dst][:, cc * 128:(cc + 1) * 128],
                                 ybf[cc][:, t * 512:(t + 1) * 512],
                                 start=(cc == 0), stop=(cc == 3)),
                             wait=wt0 if cc == 0 else [], inc=sP)
                    if _pc < 3:
                        continue
                    if _os.environ.get("PROJB") == "0":
                        b.op("vector",
                             lambda e, oc=oc, t=t, bank=bank: e.scalar_tensor_tensor(
                                 out=stream[oc][:, t * 512:(t + 1) * 512],
                                 in0=bank[:, :], scalar=1.0,
                                 in1=stream[oc][:, t * 512:(t + 1) * 512],
                                 op0=ALU.mult, op1=ALU.add),
                             wait=[(sP, sP.n)], inc=sV)
                    elif r == 0:
                        b.op("vector",
                             lambda e, oc=oc, t=t, bank=bank: e.scalar_tensor_tensor(
                                 out=stream[oc][:, t * 512:(t + 1) * 512],
                                 in0=bank[:, :], scalar=bproj_sb[:, oc:oc + 1],
                                 in1=stream[oc][:, t * 512:(t + 1) * 512],
                                 op0=ALU.add, op1=ALU.add),
                             wait=[(sP, sP.n)], inc=sV)
                    else:
                        b.op("vector",
                             lambda e, oc=oc, t=t, bank=bank: e.scalar_tensor_tensor(
                                 out=stream[oc][:, t * 512:(t + 1) * 512],
                                 in0=bank[:, :], scalar=1.0,
                                 in1=stream[oc][:, t * 512:(t + 1) * 512],
                                 op0=ALU.mult, op1=ALU.add),
                             wait=[(sP, sP.n)], inc=sV)
                    st["pj_bfree"][t % 2] = sV.at()
                st["pbf_free"][dst] = sP.at()
            st["pj_bfree"] = [None, None]
            st["ybf_free"] = sP.at()
            st["qk_banks"] = sV.at()
            st["x_free"] = sA.at()

        steps = []
        for r in range(2):
            steps.append(lambda r=r: emit_vb(r))
            steps.append(lambda r=r: emit_qk_w(4 * r))
            steps.append(lambda r=r: emit_qk_main(4 * r))
            for p in range(4 * r, 4 * r + 4):
                if p % 4 != 3:
                    steps.append(lambda p=p: emit_qk_w(p + 1))
                steps.append(lambda p=p: emit_att(p))
                if p % 4 != 3:
                    steps.append(lambda p=p: emit_qk_main(p + 1))
            steps.append(lambda r=r: emit_proj(r))
        import os as _os
        for _i, s_ in enumerate(steps):
            if _i < int(_os.environ.get("L0CUT", "99")):
                s_()
        bar()
        if _os.environ.get("ATTDBG") == "1":
            taps = (("d_qa", qa), ("d_qb", qb), ("d_ka", ka), ("d_kb", kb),
                    ("d_rows", rows), ("d_rows2", rows2), ("d_ybf0", ybf[0]),
                    ("d_ebuf", ebuf), ("d_hlr", hlr), ("d_wbf", wbf))
            for nm, sb in taps:
                hh_ = nc.dram_tensor(nm, list(sb.shape), sb.dtype,
                                     kind="ExternalOutput").ap()
                b.dma("gpsimd", hh_, sb[:], inc=sG)
            hh_ = nc.dram_tensor("d_vtm", [128, 16 * 520], BF16,
                                 kind="ExternalOutput").ap()
            b.dma("gpsimd", hh_, vtm[:], inc=sG)
            hh_ = nc.dram_tensor("d_rws", [1, 1048], F32,
                                 kind="ExternalOutput").ap()
            b.dma("gpsimd", hh_, rws[:], inc=sG)
            hh_ = nc.dram_tensor("d_dbc0", [64, 512], F32,
                                 kind="ExternalOutput").ap()
            b.dma("gpsimd", hh_, dbc[0][:], inc=sG)
            bar()

    def emit_moe():
        pool = Pool()
        hid = [pool.t(f"m_hid{i}", [128, T], BF16) for i in range(8)]
        wdh = [pool.t(f"m_wdh{i}", [128, D], BF16) for i in range(8)]
        wgc = [pool.t(f"m_wgc{i}", [128, D], BF16) for i in range(2)]
        wuc = [pool.t(f"m_wuc{i}", [128, D], BF16) for i in range(2)]
        stg_g = pool.t("m_sg", [128, D], F32)
        stg_u = pool.t("m_su", [128, D], F32)
        stg_d = pool.t("m_sd", [128, D], F32)
        wbc = pool.t("m_wbc", [128, T], BF16)
        silu_g = pool.t("m_silu", [128, T], BF16)
        gwall = pool.t("m_gwall", [128, 64], F32)
        g1row = pool.t("m_g1", [1, 8], F32)
        rcol = pool.t("m_rcol", [128, 1], F32)
        mgrow = pool.t("m_mgrow", [128, 8], F32)
        routs = pool.t("m_routs", [128, 64], F32)
        wpeT = pool.t("m_wpeT", [8, T], BF16)
        wpe_bf = pool.t("m_wpebf", [128, 8], BF16)

        gwt = _chunks(W["gate_w"])
        for c in range(NCHUNK):
            b.dma("sync", gwall[:, c * 8:(c + 1) * 8], gwt[c], inc=sL)
        gwld = sL.at()
        for c in range(NCHUNK):
            b.op("tensor",
                 lambda e, c=c: e.matmul(pbk[7][0:1, 0:8], ones_col_f[:],
                                         gwall[:, c * 8:(c + 1) * 8],
                                         start=(c == 0), stop=(c == NCHUNK - 1)),
                 wait=[gwld] if c == 0 else [], inc=sP)
        b.op("scalar",
             lambda e: e.activation(out=g1row[:], in_=pbk[7][0:1, 0:8], func=AF.Copy),
             wait=[(sP, sP.n)], inc=sA)
        g1d = sA.at()

        P_ = routs[:, 0:8]
        TOP = routs[:, 8:16]
        IS1 = routs[:, 16:24]
        IS2 = routs[:, 24:32]
        WPE = routs[:, 32:40]
        LG = routs[:, 40:48]
        SC = routs[:, 48:56]
        for t16 in range(NT128):
            for c in range(NCHUNK):
                b.op("tensor",
                     lambda e, c=c, t16=t16: e.matmul(
                         pbk[0][0:128, 0:8],
                         stream[c][:, t16 * 128:(t16 + 1) * 128],
                         gwall[:, c * 8:(c + 1) * 8],
                         start=(c == 0), stop=(c == NCHUNK - 1)),
                     wait=[gwld, (sV, sV.n)] if c == 0 else [], inc=sP)
            b.op("tensor",
                 lambda e, t16=t16: e.matmul(
                     pbk[1][0:128, 0:8],
                     statr[0:1, t16 * 128:(t16 + 1) * 128], g1row[:],
                     start=True, stop=True),
                 wait=[g1d, (sA, sA.n)], inc=sP)
            b.op("tensor",
                 lambda e, t16=t16: e.matmul(
                     pbk[2][0:128, 0:1],
                     statr[64:65, t16 * 128:(t16 + 1) * 128], ones65_f[64:65, 0:1],
                     start=True, stop=True),
                 inc=sP)
            mmd = sP.at()
            b.op("scalar",
                 lambda e: e.activation(out=mgrow[:], in_=pbk[1][0:128, 0:8], func=AF.Copy),
                 wait=[mmd], inc=sA)
            b.op("scalar",
                 lambda e: e.activation(out=rcol[:], in_=pbk[2][0:128, 0:1], func=AF.Copy),
                 inc=sA)
            evd = sA.at()
            b.op("vector",
                 lambda e: e.tensor_sub(LG, pbk[0][0:128, 0:8], mgrow[:]),
                 wait=[mmd, evd], inc=sV)
            b.op("vector", lambda e: e.tensor_scalar_mul(LG, LG, rcol[:, 0:1]), inc=sV)
            b.op("vector", lambda e: e.reduce_max(SC[:, 0:1], LG, axis=AX.X),
                 wait=[sV.at()], inc=sV)
            b.op("vector",
                 lambda e: e.tensor_scalar_mul(SC[:, 1:2], SC[:, 0:1], -1.0),
                 wait=[sV.at()], inc=sV)
            vd = sV.at()
            b.op("scalar",
                 lambda e: e.activation(out=P_, in_=LG, func=AF.Exp,
                                        bias=SC[:, 1:2], accum_out=SC[:, 2:3]),
                 wait=[vd], inc=sA)
            b.op("vector", lambda e: e.reciprocal(SC[:, 3:4], SC[:, 2:3]),
                 wait=[sA.at()], inc=sV)
            b.op("vector", lambda e: e.tensor_scalar_mul(P_, P_, SC[:, 3:4]),
                 wait=[sV.at()], inc=sV)
            b.op("vector", lambda e: e.max(out=TOP, in_=P_),
                 wait=[sV.at()], inc=sV)
            topd = sV.at()
            b.op("vector",
                 lambda e: e.tensor_single_scalar(IS1, P_, TOP[:, 0:1], op=ALU.is_equal),
                 wait=[topd], inc=sV)
            b.op("vector",
                 lambda e: e.tensor_single_scalar(IS2, P_, TOP[:, 1:2], op=ALU.is_equal),
                 inc=sV)
            b.op("vector",
                 lambda e: e.tensor_add(SC[:, 4:5], TOP[:, 0:1], TOP[:, 1:2]), inc=sV)
            b.op("vector",
                 lambda e: e.tensor_scalar_add(SC[:, 4:5], SC[:, 4:5], 1e-20), inc=sV)
            b.op("vector", lambda e: e.reciprocal(SC[:, 5:6], SC[:, 4:5]),
                 wait=[sV.at()], inc=sV)
            b.op("vector", lambda e: e.tensor_add(WPE, IS1, IS2),
                 wait=[sV.at()], inc=sV)
            b.op("vector", lambda e: e.tensor_mul(WPE, WPE, P_), inc=sV)
            b.op("vector", lambda e: e.tensor_scalar_mul(WPE, WPE, SC[:, 5:6]), inc=sV)
            b.op("scalar",
                 lambda e: e.activation(out=wpe_bf[:], in_=WPE, func=AF.Copy),
                 wait=[(sV, sV.n)], inc=sA)
            b.op("tensor",
                 lambda e: e.matmul(pbk[3][0:8, 0:128], wpe_bf[:], iden_bf[:],
                                    start=True, stop=True),
                 wait=[(sA, sA.n)], inc=sP)
            b.op("scalar",
                 lambda e, t16=t16: e.activation(
                     out=wpeT[:, t16 * 128:(t16 + 1) * 128],
                     in_=pbk[3][0:8, 0:128], func=AF.Copy),
                 wait=[(sP, sP.n)], inc=sA)
        bar()

        mstate = {"stg": None, "dstg": None}
        bfree = [None] * 8
        silu_free = [None] * NT
        wbuf_free = [None, None]
        wdh_free = None
        for ex in range(9):
            shared = (ex == 8)
            if not shared:
                wg_t = _chunks(W["w_gate"][ex])
                wu_t = _chunks(W["w_up"][ex])
                wd_t = _chunks(W["w_down"][ex])
            else:
                wg_t = _chunks(W["ws_gate"])
                wu_t = _chunks(W["ws_up"])
                wd_t = _chunks(W["ws_down"])

            wbc_done = None
            if not shared:
                for t in range(NT):
                    wt = [bfree[7]] if bfree[7] else []
                    b.op("tensor",
                         lambda e, ex=ex, t=t: e.matmul(
                             pbk[7][0:128, :], sel_sb[:, ex * 128:(ex + 1) * 128],
                             wpeT[:, t * 512:(t + 1) * 512],
                             start=True, stop=True),
                         wait=wt, inc=sP)
                    b.op("scalar",
                         lambda e, t=t: e.activation(
                             out=wbc[:, t * 512:(t + 1) * 512],
                             in_=pbk[7][0:128, :], func=AF.Copy),
                         wait=[(sP, sP.n), (sV, sV.n)], inc=sA)
                    bfree[7] = (sA, sA.n)
                wbc_done = sA.at()

            for half in range(2):
                for dc8 in range(8):
                    dc = half * 8 + dc8
                    buf = dc8 % 2
                    wt = [mstate["stg"]] if mstate["stg"] else []
                    b.dma("sync",
                          stg_g[:].rearrange("p (c n) -> p c n", n=128),
                          wg_t[:, :, dc * 128:(dc + 1) * 128].rearrange("c p n -> p c n"),
                          wait=wt, inc=sL)
                    b.dma("sync",
                          stg_u[:].rearrange("p (c n) -> p c n", n=128),
                          wu_t[:, :, dc * 128:(dc + 1) * 128].rearrange("c p n -> p c n"),
                          inc=sL)
                    ld = sL.at()
                    cwt = [ld] + ([wbuf_free[buf]] if wbuf_free[buf] else [])
                    b.op("scalar",
                         lambda e, buf=buf: e.activation(out=wgc[buf][:],
                                                         in_=stg_g[:], func=AF.Copy),
                         wait=cwt, inc=sA)
                    b.op("scalar",
                         lambda e, buf=buf: e.activation(out=wuc[buf][:],
                                                         in_=stg_u[:], func=AF.Copy),
                         inc=sA)
                    mstate["stg"] = sA.at()
                    castd = sA.at()
                    gstop = {}
                    for c in range(NCHUNK):
                        for t in range(NT):
                            wt = []
                            if c == 0:
                                wt = [bfree[t]] if bfree[t] else []
                                if t == 0:
                                    wt = wt + [castd]
                            b.op("tensor",
                                 lambda e, c=c, t=t, buf=buf: e.matmul(
                                     pbk[t][:, :], wgc[buf][:, c * 128:(c + 1) * 128],
                                     lnx[c][:, t * 512:(t + 1) * 512],
                                     start=(c == 0), stop=(c == NCHUNK - 1)),
                                 wait=wt, inc=sP)
                            if c == NCHUNK - 1:
                                gstop[t] = (sP, sP.n)
                    for t in range(NT):
                        wt = [gstop[t]] + ([silu_free[t]] if silu_free[t] else [])
                        b.op("scalar",
                             lambda e, t=t: e.activation(
                                 out=silu_g[:, t * 512:(t + 1) * 512],
                                 in_=pbk[t][:, :], func=AF.Silu),
                             wait=wt, inc=sA)
                        bfree[t] = (sA, sA.n)
                    silud = {t: bfree[t] for t in range(NT)}
                    ustop = {}
                    for c in range(NCHUNK):
                        for t in range(NT):
                            wt = []
                            if c == 0:
                                wt = [bfree[4 + t]] if bfree[4 + t] else []
                            b.op("tensor",
                                 lambda e, c=c, t=t, buf=buf: e.matmul(
                                     pbk[4 + t][:, :], wuc[buf][:, c * 128:(c + 1) * 128],
                                     lnx[c][:, t * 512:(t + 1) * 512],
                                     start=(c == 0), stop=(c == NCHUNK - 1)),
                                 wait=wt, inc=sP)
                            if c == NCHUNK - 1:
                                ustop[t] = (sP, sP.n)
                    wbuf_free[buf] = (sP, sP.n)
                    for t in range(NT):
                        if shared:
                            b.op("vector",
                                 lambda e, t=t, dc8=dc8: e.tensor_mul(
                                     hid[dc8][:, t * 512:(t + 1) * 512],
                                     pbk[4 + t][:, :],
                                     silu_g[:, t * 512:(t + 1) * 512]),
                                 wait=[ustop[t], silud[t]], inc=sV)
                        else:
                            b.op("vector",
                                 lambda e, t=t, dc8=dc8: e.tensor_mul(
                                     hid[dc8][:, t * 512:(t + 1) * 512],
                                     pbk[4 + t][:, :],
                                     wbc[:, t * 512:(t + 1) * 512]),
                                 wait=[ustop[t], silud[t], wbc_done], inc=sV)
                            b.op("vector",
                                 lambda e, t=t, dc8=dc8: e.tensor_mul(
                                     hid[dc8][:, t * 512:(t + 1) * 512],
                                     hid[dc8][:, t * 512:(t + 1) * 512],
                                     silu_g[:, t * 512:(t + 1) * 512]),
                                 inc=sV)
                        bfree[4 + t] = (sV, sV.n)
                        silu_free[t] = (sV, sV.n)
                hidd = sV.at()

                for dc8 in range(8):
                    dc = half * 8 + dc8
                    wt = [mstate["dstg"]] if mstate["dstg"] else []
                    b.dma("gpsimd", stg_d[:], wd_t[dc], wait=wt, inc=sG)
                    cwt = [sG.at()] + ([wdh_free] if wdh_free else [])
                    b.op("scalar",
                         lambda e, dc8=dc8: e.activation(out=wdh[dc8][:],
                                                         in_=stg_d[:], func=AF.Copy),
                         wait=cwt, inc=sA)
                    mstate["dstg"] = sA.at()
                wdc = sA.at()

                for oc in range(NCHUNK):
                    for t in range(NT):
                        bi = t % 4
                        bank = pbk[bi]
                        for dc8 in range(8):
                            wt = []
                            if dc8 == 0:
                                wt = [bfree[bi]] if bfree[bi] else []
                                if oc == 0 and t == 0:
                                    wt = wt + [wdc, hidd]
                            b.op("tensor",
                                 lambda e, oc=oc, t=t, dc8=dc8, bank=bank: e.matmul(
                                     bank[:, :], wdh[dc8][:, oc * 128:(oc + 1) * 128],
                                     hid[dc8][:, t * 512:(t + 1) * 512],
                                     start=(dc8 == 0), stop=(dc8 == 7)),
                                 wait=wt, inc=sP)
                        dstop = (sP, sP.n)
                        b.op("vector",
                             lambda e, oc=oc, t=t, bank=bank: e.scalar_tensor_tensor(
                                 out=stream[oc][:, t * 512:(t + 1) * 512],
                                 in0=bank[:, :], scalar=1.0,
                                 in1=stream[oc][:, t * 512:(t + 1) * 512],
                                 op0=ALU.mult, op1=ALU.add),
                             wait=[dstop], inc=sV)
                        bfree[bi] = (sV, sV.n)
                wdh_free = (sP, sP.n)
        bar()

    def emit_l1_attn():
        pool = Pool()
        kfm = [pool.t(f"b_k{c}", [128, T], BF16) for c in range(NCHUNK)]
        vtm1 = pool.t("b_vtm", [128, 16 * 1024], BF16)
        rows1 = pool.t("b_rows", [64, T], BF16)
        rws1 = pool.t("b_rws", [1, 1024], F32)
        sq1 = pool.t("b_sq", [128, 2 * 512], BF16)
        cb = pool.t("b_cb", [128, 1], F32)
        tmps = pool.t("b_tmps", [128, 256], F32)
        avsb = pool.t("b_avsb", [128, 256], F32)
        hlr1 = pool.t("b_hlr", [1, 512], BF16)
        wstg = pool.t("b_wstg", [128, 1024], F32)
        wstg2 = pool.t("b_wstg2", [128, 512], F32)
        wbf = [pool.t(f"b_wbf{i}", [128, 1024], BF16) for i in range(2)]
        pX = pool.fork()
        wvbf_a = pX.t("b_wvbf", [128, 8 * 256], BF16)
        wvbf_b = pX.t("b_wvbf2", [128, 8 * 256], BF16)
        pY2 = pool.fork()
        qfm = pY2.t("b_qfm", [128, 8 * 256], BF16)
        ET = pY2.t("b_et", [128, 3 * 256], BF16)
        dbc1 = [pY2.t("b_dbc0", [128, 256], F32)]

        import os as _os
        _lc = int(_os.environ.get("L1CUT", "99"))
        st = {"stg": None, "buf": [None, None], "x_free": None}

        def stream_w(wap, oc, buf, eng="scalar"):
            wt = [st["stg"]] if st["stg"] else []
            b.dma("sync",
                  wstg[:].rearrange("p (c n) -> p c n", n=128),
                  _chunks(wap)[:, :, oc * 128:(oc + 1) * 128].rearrange("c p n -> p c n"),
                  wait=wt, inc=sL)
            cwt = [sL.at()] + ([st["buf"][buf]] if st["buf"][buf] else [])
            if eng == "scalar":
                b.op("scalar",
                     lambda e, buf=buf: e.activation(out=wbf[buf][:], in_=wstg[:],
                                                     func=AF.Copy),
                     wait=cwt, inc=sA)
                st["stg"] = sA.at()
                return sA.at()
            b.op("vector",
                 lambda e, buf=buf: e.tensor_copy(wbf[buf][:], wstg[:]),
                 wait=cwt, inc=sV)
            st["stg"] = sV.at()
            return sV.at()

        kv_bfree = [None, None]
        kev = {}
        for oc in range(NCHUNK if _lc >= 1 else 0):
            buf = oc % 2
            castd = stream_w(W["wk"], oc, buf)
            for t in range(NT):
                bank = pbk[t % 2]
                for c in range(NCHUNK):
                    wt = []
                    if c == 0:
                        wt = ([kv_bfree[t % 2]] if kv_bfree[t % 2] else []) + [castd]
                    b.op("tensor",
                         lambda e, c=c, t=t, buf=buf, bank=bank: e.matmul(
                             bank[:, :], wbf[buf][:, c * 128:(c + 1) * 128],
                             lnx[c][:, t * 512:(t + 1) * 512],
                             start=(c == 0), stop=(c == NCHUNK - 1)),
                         wait=wt, inc=sP)
                b.op("scalar",
                     lambda e, oc=oc, t=t, bank=bank: e.activation(
                         out=kfm[oc][:, t * 512:(t + 1) * 512], in_=bank[:, :],
                         func=AF.Copy),
                     wait=[(sP, sP.n)], inc=sA)
                kv_bfree[t % 2] = (sA, sA.n)
                kev[(oc, t)] = sA.at()
            st["buf"][buf] = (sP, sP.n)
        def emit_kcalib():
            mmk = {}
            for i, (t, oc) in enumerate([(t, oc) for t in range(NT)
                                         for oc in range(NCHUNK)]):
                slot = sq1[:, (i % 2) * 512:(i % 2) * 512 + 512]
                sw = [kev[(oc, t)]]
                if i >= 2:
                    sw.append(mmk[i - 2])
                b.op("vector",
                     lambda e, oc=oc, t=t, slot=slot: e.tensor_mul(
                         slot, kfm[oc][:, t * 512:(t + 1) * 512],
                         kfm[oc][:, t * 512:(t + 1) * 512]),
                     wait=sw, inc=sV)
                b.op("tensor",
                     lambda e, t=t, oc=oc, slot=slot: e.matmul(
                         pbk[4 + t][0:1, 0:512], onesc_bf[:, :], slot,
                         start=(oc == 0), stop=(oc == NCHUNK - 1)),
                     wait=[(sV, sV.n)], inc=sP)
                mmk[i] = sP.at()
            for t in range(NT):
                b.op("scalar",
                     lambda e, t=t: e.activation(
                         out=rows1[0:1, t * 512:(t + 1) * 512],
                         in_=pbk[4 + t][0:1, 0:512], func=AF.Sqrt),
                     wait=[(sP, sP.n)], inc=sA)
            b.op("vector",
                 lambda e: e.reduce_max(rws1[0:1, 800:801], rows1[0:1, 0:T], axis=AX.X),
                 wait=[(sA, sA.n)], inc=sV)

        castd_h = {}
        wv_read = {}

        def wv_load(q4):
            wvb = (wvbf_a, wvbf_b)[q4 % 2]
            cw0 = [wv_read[q4 - 2]] if (q4 - 2) in wv_read else []
            for qtr in range(4):
                wt = [st["wv_stg"]] if st.get("wv_stg") else []
                b.dma("sync", wstg2[:].rearrange("p (c n) -> p c n", n=64),
                      _chunks(W["wv"])[:, :, q4 * 256 + qtr * 64:
                                       q4 * 256 + (qtr + 1) * 64].rearrange("c p n -> p c n"),
                      wait=wt, inc=sL)
                b.op("vector",
                     lambda e, qtr=qtr, wvb=wvb: e.tensor_copy(
                         wvb[:].rearrange("p (c n) -> p c n", n=256)[
                             :, :, qtr * 64:(qtr + 1) * 64],
                         wstg2[:].rearrange("p (c n) -> p c n", n=64)),
                     wait=(cw0 if qtr == 0 else []) + [sL.at()], inc=sV)
                st["wv_stg"] = sV.at()
            castd_h[q4] = sV.at()
        if _lc >= 2:
            wv_load(0)
            wv_load(1)
        vb_free = [None, None]
        for q4 in range(4 if _lc >= 2 else 0):
            wvb = (wvbf_a, wvbf_b)[q4 % 2]
            castd = castd_h[q4]
            for t16 in range(NT128):
                bank = pbk[2 + t16 % 2]
                wt0 = [castd]
                if q4 == 0 and t16 == 0:
                    wt0.append((sP, sP.n))
                if vb_free[t16 % 2]:
                    wt0.append(vb_free[t16 % 2])
                for c in range(NCHUNK):
                    b.op("tensor",
                         lambda e, c=c, t16=t16, bank=bank, wvb=wvb: e.matmul(
                             bank[0:128, 0:256], lnx[c][:, t16 * 128:(t16 + 1) * 128],
                             wvb[:, c * 256:(c + 1) * 256],
                             start=(c == 0), stop=(c == NCHUNK - 1)),
                         wait=wt0 if c == 0 else [], inc=sP)
                b.op("scalar",
                     lambda e, t16=t16, q4=q4, bank=bank: e.activation(
                         out=vtm1[:, t16 * 1024 + q4 * 256:t16 * 1024 + (q4 + 1) * 256],
                         in_=bank[0:128, 0:256], func=AF.Copy),
                     wait=[(sP, sP.n)], inc=sA)
                vb_free[t16 % 2] = (sA, sA.at()[1])
            wv_read[q4] = (sP, sP.n)
            if _lc >= 2 and q4 + 2 < 4:
                wv_load(q4 + 2)
            if q4 == 0 and _lc >= 1:
                emit_kcalib()
        if _lc >= 1 and _lc < 2:
            emit_kcalib()
        kmax_d = sV.at()
        vtm_d = sA.at()
        wv_read_done = sP.at()

        l1s = {"av_free": None, "bc_free": None, "qfm_free": None,
               "cb_free": None, "dbc1_free": None, "pend": None}

        def emit_epi1(qg, hilo):
            bw = [hilo]
            if l1s["bc_free"]:
                bw.append(l1s["bc_free"])
            b.op("tensor",
                 lambda e: e.matmul(pbk[7][0:128, 0:256], onesr_bf[0:1, 0:128],
                                    hlr1[0:1, 0:256], start=True, stop=False),
                 wait=bw, inc=sP)
            b.op("tensor",
                 lambda e: e.matmul(pbk[7][0:128, 0:256], onesr_bf[0:1, 0:128],
                                    hlr1[0:1, 256:512], start=False, stop=True),
                 inc=sP)
            dw = [(sP, sP.n)]
            if l1s["dbc1_free"]:
                dw.append(l1s["dbc1_free"])
            b.op("scalar",
                 lambda e: e.activation(out=dbc1[0][:, :],
                                        in_=pbk[7][0:128, 0:256], func=AF.Copy),
                 wait=dw, inc=sA)
            l1s["bc_free"] = sA.at()
            l1s["cb_free"] = sA.at()
            for oc in range(NCHUNK):
                avb = pbk[2 + oc // 2]
                aoff = (oc % 2) * 256
                b.op("scalar",
                     lambda e, oc=oc, avb=avb, aoff=aoff: e.activation(
                         out=avsb[:, :], in_=avb[0:128, aoff:aoff + 256], func=AF.Copy),
                     wait=[(sA, sA.n), (sV, max(0, sV.n - 1))], inc=sA)
                b.op("vector",
                     lambda e, oc=oc, qg=qg: e.tensor_mul(
                         tmps[:, :], avsb[:, :], dbc1[0][:, :]),
                     wait=[(sA, sA.n)], inc=sV)
                b.op("vector",
                     lambda e, oc=oc, qg=qg: e.tensor_add(
                         stream[oc][:, qg * 256:(qg + 1) * 256],
                         stream[oc][:, qg * 256:(qg + 1) * 256], tmps[:, :]),
                     inc=sV)
            l1s["av_free"] = sV.at()
            l1s["dbc1_free"] = sV.at()

        for qg in range(min(8, max(0, _lc - 2))):
            for oc in range(NCHUNK):
                buf = oc % 2
                castd = stream_w(W["wq"], oc, buf, eng="vector")
                if qg == 0 and oc == 0:
                    castd_l = [castd, wv_read_done, (sA, sA.n)]
                else:
                    castd_l = [castd, (sA, sA.n)]
                bank = pbk[0] if oc % 2 == 0 else pbk[1]
                wt0 = list(castd_l)
                if l1s["qfm_free"]:
                    wt0.append(l1s["qfm_free"])
                for c in range(NCHUNK):
                    b.op("tensor",
                         lambda e, c=c, oc=oc, qg=qg, buf=buf, bank=bank: e.matmul(
                             bank[0:128, 0:256], wbf[buf][:, c * 128:(c + 1) * 128],
                             lnx[c][:, qg * 256:(qg + 1) * 256],
                             start=(c == 0), stop=(c == NCHUNK - 1)),
                         wait=wt0 if c == 0 else [], inc=sP)
                b.op("scalar",
                     lambda e, oc=oc, bank=bank: e.activation(
                         out=qfm[:, oc * 256:(oc + 1) * 256], in_=bank[0:128, 0:256],
                         func=AF.Copy),
                     wait=[(sP, sP.n)], inc=sA)
                st["buf"][buf] = (sP, sP.n)
                b.op("vector",
                     lambda e, oc=oc: e.tensor_mul(
                         sq1[:, 0:256], qfm[:, oc * 256:(oc + 1) * 256],
                         qfm[:, oc * 256:(oc + 1) * 256]),
                     wait=[(sA, sA.n), (sP, sP.n)], inc=sV)
                cw = [(sV, sV.n)]
                if oc == 0 and l1s["cb_free"]:
                    cw.append(l1s["cb_free"])
                b.op("tensor",
                     lambda e, oc=oc: e.matmul(
                         pbk[6][0:1, 256:512], onesc_bf[:, :], sq1[:, 0:256],
                         start=(oc == 0), stop=(oc == NCHUNK - 1)),
                     wait=cw, inc=sP)
            qfm_d = sA.at()
            if l1s["pend"]:
                emit_epi1(*l1s["pend"])
                l1s["pend"] = None
            _lq = int(_os.environ.get("L1QG", "9"))
            if _lq < 2:
                continue
            b.op("scalar",
                 lambda e: e.activation(out=rows1[32:33, 0:256], in_=pbk[6][0:1, 256:512],
                                        func=AF.Sqrt),
                 wait=[(sP, sP.n)], inc=sA)
            b.op("vector",
                 lambda e: e.reduce_max(rws1[0:1, 802:803], rows1[32:33, 0:256], axis=AX.X),
                 wait=[(sA, sA.n), kmax_d], inc=sV)
            b.op("vector",
                 lambda e: e.tensor_mul(rws1[0:1, 804:805], rws1[0:1, 802:803],
                                        rws1[0:1, 800:801]),
                 inc=sV)
            b.op("vector",
                 lambda e: e.tensor_scalar_mul(rws1[0:1, 806:807], rws1[0:1, 804:805],
                                               -1.0 / 32.0),
                 inc=sV)
            cb_mm_w = [(sV, sV.n)]
            if l1s["cb_free"]:
                cb_mm_w.append(l1s["cb_free"])
            b.op("tensor",
                 lambda e: e.matmul(pbk[7][0:128, 256:257], onesr_f[0:1, :],
                                    rws1[0:1, 806:807], start=True, stop=True),
                 wait=cb_mm_w, inc=sP)
            b.op("scalar",
                 lambda e: e.activation(out=cb[:, :], in_=pbk[7][0:128, 256:257],
                                        func=AF.Copy),
                 wait=[(sP, sP.n)], inc=sA)
            cb_d = sA.at()

            if _lq < 3:
                continue
            nk = 2 * (qg + 1)
            L = nk - 1
            expd = {}
            avdn1 = {}
            pav1 = (sP, sP.n)
            for i in range(L + 3):
                if i <= L:
                    kt = i
                    diag1 = (kt == 2 * qg)
                    diag2 = (kt == 2 * qg + 1)
                    qoff = 128 if diag2 else 0
                    w = 256 - qoff
                    scb = pbk[i % 2]
                    wt = [qfm_d]
                    if i >= 2:
                        wt.append(expd[i - 2])
                    for c in range(NCHUNK):
                        b.op("tensor",
                             lambda e, kt=kt, qoff=qoff, w=w, scb=scb, c=c: e.matmul(
                                 scb[0:128, 0:w],
                                 kfm[c][:, kt * 128:(kt + 1) * 128],
                                 qfm[:, c * 256 + qoff:c * 256 + 256],
                                 start=(c == 0), stop=(c == NCHUNK - 1)),
                             wait=wt if c == 0 else [], inc=sP)
                    scd = sP.at()
                    if diag1 or diag2:
                        b.op("vector",
                             lambda e, scb=scb: e.tensor_add(
                                 scb[0:128, 0:128], scb[0:128, 0:128], tri_sb[:]),
                             wait=[scd], inc=sV)
                    ew = [scd, cb_d] + ([(sV, sV.n)] if (diag1 or diag2) else [])
                    ew.append(avdn1[i - 3] if i >= 3 else pav1)
                    b.op("scalar",
                         lambda e, i=i, w=w, scb=scb: e.activation(
                             out=ET[:, (i % 3) * 256:(i % 3) * 256 + w],
                             in_=scb[0:128, 0:w], func=AF.Exp,
                             scale=1.0 / 32.0, bias=cb[:, 0:1]),
                         wait=ew, inc=sA)
                    expd[i] = sA.at()
                j = i - 1
                if _lq < 4:
                    j = -1
                if 0 <= j <= L:
                    kt = j
                    qoff = 128 if (kt == 2 * qg + 1) else 0
                    w = 256 - qoff
                    wt = [expd[j]]
                    if j == 0 and l1s["av_free"]:
                        wt.append(l1s["av_free"])
                    for oc in range(NCHUNK):
                        avb = pbk[2 + oc // 2]
                        aoff = (oc % 2) * 256
                        b.op("tensor",
                             lambda e, kt=kt, qoff=qoff, w=w, j=j, L=L, oc=oc, avb=avb, aoff=aoff: e.matmul(
                                 avb[0:128, aoff + qoff:aoff + 256],
                                 vtm1[:, kt * 1024 + oc * 128:kt * 1024 + (oc + 1) * 128],
                                 ET[:, (kt % 3) * 256:(kt % 3) * 256 + w],
                                 start=(j == 0 and oc % 2 == 0), stop=(j == L)),
                             wait=wt if oc == 0 else [], inc=sP)
                    b.op("tensor",
                         lambda e, kt=kt, qoff=qoff, w=w, j=j, L=L: e.matmul(
                             pbk[6][0:1, qoff:256], onesc_bf[:, :],
                             ET[:, (kt % 3) * 256:(kt % 3) * 256 + w],
                             start=(j == 0), stop=(j == L)),
                         inc=sP)
                    avdn1[j] = sP.at()
            avd = sP.at()
            if _lq < 5:
                continue
            b.op("vector",
                 lambda e: e.reciprocal(rws1[0:1, 0:256], pbk[6][0:1, 0:256]),
                 wait=[avd], inc=sV)
            b.op("scalar",
                 lambda e: e.activation(out=hlr1[0:1, 0:256], in_=rws1[0:1, 0:256],
                                        func=AF.Copy),
                 wait=[(sV, sV.n)], inc=sA)
            b.op("vector",
                 lambda e: e.tensor_sub(rws1[0:1, 512:768], rws1[0:1, 0:256],
                                        hlr1[0:1, 0:256]),
                 wait=[(sA, sA.n)], inc=sV)
            b.op("scalar",
                 lambda e: e.activation(out=hlr1[0:1, 256:512], in_=rws1[0:1, 512:768],
                                        func=AF.Copy),
                 wait=[(sV, sV.n)], inc=sA)
            hilo = sA.at()
            l1s["pend"] = (qg, hilo)
            l1s["qfm_free"] = sP.at()
        if l1s["pend"]:
            emit_epi1(*l1s["pend"])
            l1s["pend"] = None
        bar()
        if _os.environ.get("ATTDBG") == "1":
            hh_ = nc.dram_tensor("d_et", [128, 3 * 256], BF16,
                                 kind="ExternalOutput").ap()
            b.dma("gpsimd", hh_, ET[:], inc=sG)
            hh_ = nc.dram_tensor("d_cb", [128, 1], F32,
                                 kind="ExternalOutput").ap()
            b.dma("gpsimd", hh_, cb[:], inc=sG)
            hh_ = nc.dram_tensor("d_dbc1", [128, 256], F32,
                                 kind="ExternalOutput").ap()
            b.dma("gpsimd", hh_, dbc1[0][:], inc=sG)
            hh_ = nc.dram_tensor("d_vtm1", [128, 16 * 1024], BF16,
                                 kind="ExternalOutput").ap()
            b.dma("gpsimd", hh_, vtm1[:], inc=sG)
            hh_ = nc.dram_tensor("d_kfm0", [128, T], BF16,
                                 kind="ExternalOutput").ap()
            b.dma("gpsimd", hh_, kfm[1][:], inc=sG)
            bar()

    def emit_l1_ffn():
        pool = Pool()
        hid = [pool.t(f"f_h{i}", [128, T], BF16) for i in range(12)]
        stg_g = pool.t("f_sg_i", [128, 1024], I32)
        stg_u = pool.t("f_su_i", [128, 1024], I32)
        cgu = [pool.t(f"f_cgu{i}", [128, 2048], BF16) for i in range(2)]
        wg_b = [pool.t(f"f_wg{i}", [128, 8 * 128], BF16) for i in range(2)]
        wu_b = [pool.t(f"f_wu{i}", [128, 8 * 128], BF16) for i in range(2)]
        sg = [pool.t(f"f_sgl{i}", [128, 512], BF16) for i in range(2)]
        stg_d = pool.t("f_sd_i", [128, 1536], I32)
        wdc = [pool.t(f"f_wdc{i}", [128, 1536], BF16) for i in range(2)]
        wd_b = [pool.t(f"f_wd{i}", [128, 12 * 128], BF16) for i in range(2)]

        st = {"stg": None, "cgu_free": [None, None], "wb_free": [None, None],
              "tb_free": [None, None], "dstg": None, "wdc_free": [None, None],
              "wdb_free": [None, None], "cast": {}, "dcast": {}}
        bfree = [None] * 8
        sg_free = [None, None]
        hid_free = {}

        def gu_load(g, hl):
            h = g * 12 + hl
            wt = [st["stg"]] if st["stg"] else []
            b.dma("sync", stg_g[:], _chunks(W["wg_i"])[h], wait=wt, inc=sL)
            b.dma("sync", stg_u[:], _chunks(W["wu_i"])[h], inc=sL)
            ld = sL.at()
            buf = hl % 2
            cw = [ld] + ([st["cgu_free"][buf]] if st["cgu_free"][buf] else [])
            b.op("vector",
                 lambda e, buf=buf: e.tensor_copy(cgu[buf][:, 0:1024], stg_g[:]),
                 wait=cw, inc=sV)
            b.op("vector",
                 lambda e, buf=buf: e.tensor_copy(cgu[buf][:, 1024:2048], stg_u[:]),
                 inc=sV)
            st["stg"] = sV.at()
            st["cast"][hl] = sV.at()

        def gu_transpose(hl, ccs):
            buf = hl % 2
            for cc in ccs:
                bank = pbk[4 + cc % 2]
                tw = [st["cast"][hl]] if cc == ccs[0] else []
                if st["tb_free"][cc % 2]:
                    tw.append(st["tb_free"][cc % 2])
                if cc < 2 and st["wb_free"][buf]:
                    tw.append(st["wb_free"][buf])
                b.op("tensor",
                     lambda e, cc=cc, buf=buf, bank=bank: e.matmul(
                         bank[0:128, 0:128], cgu[buf][:, cc * 128:(cc + 1) * 128],
                         iden_bf[:], start=True, stop=True),
                     wait=tw, inc=sP)
                dstb = wg_b[buf] if cc < 8 else wu_b[buf]
                dcol = (cc % 8) * 128
                b.op("scalar",
                     lambda e, dstb=dstb, dcol=dcol, bank=bank: e.activation(
                         out=dstb[:, dcol:dcol + 128], in_=bank[0:128, 0:128],
                         func=AF.Copy),
                     wait=[(sP, sP.n)], inc=sA)
                st["tb_free"][cc % 2] = sA.at()
                if cc == 15:
                    st["cgu_free"][buf] = sP.at()

        def wd_load(g, oc):
            buf = oc % 2
            wt = [st["dstg"]] if st["dstg"] else []
            b.dma("gpsimd", stg_d[:],
                  W["wd_i"][oc * 128:(oc + 1) * 128, g * 1536:(g + 1) * 1536],
                  wait=wt, inc=sG)
            cw = [sG.at()] + ([st["wdc_free"][buf]] if st["wdc_free"][buf] else [])
            b.op("vector",
                 lambda e, buf=buf: e.tensor_copy(wdc[buf][:], stg_d[:]),
                 wait=cw, inc=sV)
            st["dstg"] = sV.at()
            st["dcast"][oc] = sV.at()

        def wd_transpose(oc, js):
            buf = oc % 2
            for j in js:
                bank = pbk[4 + j % 2]
                tw = [st["dcast"][oc]] if j == js[0] else []
                if st["tb_free"][j % 2]:
                    tw.append(st["tb_free"][j % 2])
                if j < 2 and st["wdb_free"][buf]:
                    tw.append(st["wdb_free"][buf])
                b.op("tensor",
                     lambda e, j=j, buf=buf, bank=bank: e.matmul(
                         bank[0:128, 0:128], wdc[buf][:, j * 128:(j + 1) * 128],
                         iden_bf[:], start=True, stop=True),
                     wait=tw, inc=sP)
                b.op("scalar",
                     lambda e, j=j, buf=buf, bank=bank: e.activation(
                         out=wd_b[buf][:, j * 128:(j + 1) * 128],
                         in_=bank[0:128, 0:128], func=AF.Copy),
                     wait=[(sP, sP.n)], inc=sA)
                st["tb_free"][j % 2] = sA.at()
                if j == 11:
                    st["wdc_free"][buf] = sP.at()

        for g in range(2):
            gu_load(g, 0)
            gu_transpose(0, list(range(16)))
            for hl in range(12):
                buf = hl % 2
                if hl < 11:
                    gu_load(g, hl + 1)
                trd = st["tb_free"][0]
                for t in range(NT):
                    gbank = pbk[t % 2]
                    ubank = pbk[2 + t % 2]
                    for c in range(NCHUNK):
                        wt = []
                        if c == 0:
                            wt = ([bfree[t % 2]] if bfree[t % 2] else []) + [(sA, sA.n)]
                        b.op("tensor",
                             lambda e, c=c, t=t, buf=buf, gbank=gbank: e.matmul(
                                 gbank[:, :], wg_b[buf][:, c * 128:(c + 1) * 128],
                                 lnx[c][:, t * 512:(t + 1) * 512],
                                 start=(c == 0), stop=(c == NCHUNK - 1)),
                             wait=wt, inc=sP)
                    gd = (sP, sP.n)
                    swt = [gd] + ([sg_free[t % 2]] if sg_free[t % 2] else [])
                    b.op("scalar",
                         lambda e, gbank=gbank, t=t: e.activation(
                             out=sg[t % 2][:], in_=gbank[:, :], func=AF.Silu,
                             scale=1.0 / 127.0),
                         wait=swt, inc=sA)
                    silud = sA.at()
                    bfree[t % 2] = silud
                    for c in range(NCHUNK):
                        wt = []
                        if c == 0:
                            wt = [bfree[2 + t % 2]] if bfree[2 + t % 2] else []
                        b.op("tensor",
                             lambda e, c=c, t=t, buf=buf, ubank=ubank: e.matmul(
                                 ubank[:, :], wu_b[buf][:, c * 128:(c + 1) * 128],
                                 lnx[c][:, t * 512:(t + 1) * 512],
                                 start=(c == 0), stop=(c == NCHUNK - 1)),
                             wait=wt, inc=sP)
                    ud = (sP, sP.n)
                    hw = [ud, silud]
                    if hid_free.get(hl):
                        hw.append(hid_free[hl])
                    b.op("vector",
                         lambda e, hl=hl, t=t, ubank=ubank: e.scalar_tensor_tensor(
                             out=hid[hl][:, t * 512:(t + 1) * 512],
                             in0=ubank[:, :], scalar=1.0 / 127.0, in1=sg[t % 2][:],
                             op0=ALU.mult, op1=ALU.mult),
                         wait=hw, inc=sV)
                    bfree[2 + t % 2] = (sV, sV.n)
                    sg_free[t % 2] = (sV, sV.n)
                    if hl < 11:
                        gu_transpose(hl + 1, list(range(4 * t, 4 * t + 4)))
                st["wb_free"][buf] = (sP, sP.n)
            hidd = sV.at()
            wd_load(g, 0)
            wd_transpose(0, list(range(12)))
            for oc in range(NCHUNK):
                buf = oc % 2
                if oc < 7:
                    wd_load(g, oc + 1)
                for t in range(NT):
                    bank = pbk[6 + t % 2]
                    for hl in range(12):
                        wt = []
                        if hl == 0:
                            wt = ([bfree[6 + t % 2]] if bfree[6 + t % 2] else []) + [(sA, sA.n)]
                            if oc == 0 and t == 0:
                                wt.append(hidd)
                        b.op("tensor",
                             lambda e, hl=hl, t=t, buf=buf, bank=bank: e.matmul(
                                 bank[:, :], wd_b[buf][:, hl * 128:(hl + 1) * 128],
                                 hid[hl][:, t * 512:(t + 1) * 512],
                                 start=(hl == 0), stop=(hl == 11)),
                             wait=wt, inc=sP)
                    dstop = (sP, sP.n)
                    b.op("vector",
                         lambda e, oc=oc, t=t, bank=bank: e.scalar_tensor_tensor(
                             out=stream[oc][:, t * 512:(t + 1) * 512],
                             in0=bank[:, :], scalar=1.0 / 127.0,
                             in1=stream[oc][:, t * 512:(t + 1) * 512],
                             op0=ALU.mult, op1=ALU.add),
                         wait=[dstop], inc=sV)
                    bfree[6 + t % 2] = (sV, sV.n)
                    if oc < 7 and t < 3:
                        wd_transpose(oc + 1, list(range(4 * t, 4 * t + 4)))
                st["wdb_free"][buf] = (sP, sP.n)
            for hl in range(12):
                hid_free[hl] = (sP, sP.n)
        bar()

    if 0 in blocks:
        emit_ln()
        emit_l0_attn()
    if 1 in blocks:
        emit_ln()
        emit_moe()
    if 2 in blocks:
        emit_ln()
        emit_l1_attn()
    if 3 in blocks:
        emit_ln()
        emit_l1_ffn()

    pool = Pool()
    otm = [pool.t(f"otm{i}", [128, D], F32) for i in range(2)]
    st_hist = []
    for tc in range(NT128):
        buf = otm[tc % 2]
        for fc in range(NCHUNK):
            bank = pbk[fc % 2]
            b.op("tensor",
                 lambda e, fc=fc, tc=tc, bank=bank: e.matmul(
                     bank[0:128, 0:128], stream[fc][:, tc * 128:(tc + 1) * 128],
                     iden_f[:], start=True, stop=True),
                 wait=[(sA, max(0, sA.n - 1)), (sV, sV.n)], inc=sP)
            evw = [(sP, sP.n)]
            if fc == 0 and tc >= 2:
                evw.append((sG, st_hist[tc - 2]))
            b.op("scalar",
                 lambda e, fc=fc, buf=buf, bank=bank: e.activation(
                     out=buf[:, fc * 128:(fc + 1) * 128],
                     in_=bank[0:128, 0:128], func=AF.Copy),
                 wait=evw, inc=sA)
        b.dma("gpsimd", out_h[tc * 128:(tc + 1) * 128, :], buf[:],
              wait=[(sA, sA.n)], inc=sG)
        st_hist.append(sG.n)
    b.op("gpsimd", lambda e: e.engine_nop(), wait=[(sG, sG.n)])

    with nc.Block() as block:
        b.finalize(block)
    nc._b = b
    return nc


def _consts():
    ident = np.eye(128)
    tri = np.zeros((128, 128), dtype=np.float32)
    for k in range(128):
        tri[k, 0:k] = NEG
    sel = np.zeros((8, 8 * 128), dtype=np.float32)
    for e in range(8):
        sel[e, e * 128:(e + 1) * 128] = 1.0
    return {
        "c_ident_bf": ident.astype(ml_dtypes.bfloat16),
        "c_ident_f": ident.astype(np.float32),
        "c_tri": tri.astype(ml_dtypes.bfloat16),
        "c_sel": sel.astype(ml_dtypes.bfloat16),
    }


_WMAP = {
    0: ["L0_attn_w_attn", "L0_attn_b_attn", "L0_attn_w_proj", "L0_attn_b_proj"],
    1: ["L0_ffn_gate_w", "L0_ffn_w_gate", "L0_ffn_w_up", "L0_ffn_w_down",
        "L0_ffn_ws_gate", "L0_ffn_ws_up", "L0_ffn_ws_down"],
    2: ["L1_attn_Wk", "L1_attn_Wq", "L1_attn_Wv"],
    3: ["L1_ffn_w_gate", "L1_ffn_w_up", "L1_ffn_w_down"],
}


def kernel(**inputs):
    consts = _consts()
    x = np.ascontiguousarray(inputs["L0_attn_x"], dtype=np.float32)
    launches = [[0, 1, 2, 3]] if FUSED else [[0], [1], [2], [3]]
    cur = [x[c] for c in range(8)]
    for blocks in launches:
        nc = build_program(blocks)
        names = [n for blk in blocks for n in _WMAP[blk]]
        wmap = {n: np.ascontiguousarray(inputs[n]) for n in names}
        in_maps = [dict(x=np.ascontiguousarray(cur[c]), **wmap, **consts)
                   for c in range(8)]
        res = run_bass_kernel_spmd(nc, in_maps, core_ids=list(range(8)))
        cur = [res.results[c]["out"] for c in range(8)]
    return np.stack(cur).astype(np.float32)
